# version 19
# baseline (speedup 1.0000x reference)
import numpy as np
import ml_dtypes
from contextlib import ExitStack

import concourse.bass as bass
import concourse.mybir as mybir
from concourse.bass_utils import run_bass_kernel_spmd

F32 = mybir.dt.float32
BF16 = mybir.dt.bfloat16
AF = mybir.ActivationFunctionType
ALU = mybir.AluOpType
AX = mybir.AxisListType
bfnp = ml_dtypes.bfloat16

B, S, D = 4, 4096, 2048
E = 16
NCORE = 8
NEG = -1.0e30
QSCALE = 1.0 / float(np.sqrt(128.0))
SLOPES = [float(2.0 ** (-(i + 1) / 2.0)) for i in range(16)]


class Prog:
    ENGINES = ("sync", "tensor", "vector", "scalar", "gpsimd")

    def __init__(self, nc, stack):
        self.nc = nc
        self.stack = stack
        self.items = {e: [] for e in self.ENGINES}
        self._sems = {}
        self.counts = {}
        self.last_wait = {e: {} for e in self.ENGINES}

    def sem(self, name):
        if name not in self._sems:
            self._sems[name] = self.stack.enter_context(self.nc.semaphore(name))
            self.counts[name] = 0
        return self._sems[name]

    def wait(self, eng, tok):
        if tok is None:
            return
        name, val = tok
        if val <= 0:
            return
        if val <= self.last_wait[eng].get(name, 0):
            return
        self.last_wait[eng][name] = val
        self.items[eng].append(("w", self.sem(name), val))

    def op(self, eng, fn, waits=(), inc=None, incby=1):
        for tok in waits:
            self.wait(eng, tok)
        if inc is not None:
            s = self.sem(inc)
            self.counts[inc] += incby
            self.items[eng].append(("o", fn, s, incby))
            return (inc, self.counts[inc])
        self.items[eng].append(("o", fn, None, 0))
        return None

    def dma(self, fn, stream, waits=()):
        return self.op("sync", fn, waits=waits, inc=stream, incby=16)

    def gdma(self, fn, stream, waits=()):
        return self.op("gpsimd", fn, waits=waits, inc=stream, incby=16)

    def now(self, name):
        return (name, self.counts.get(name, 0))

    def barrier(self, exclude=()):
        toks = [self.now(n) for n in self._sems]
        for eng in self.ENGINES:
            if eng in exclude:
                continue
            for tok in toks:
                self.wait(eng, tok)

    def replay(self):
        with self.nc.Block() as block:
            for eng in self.ENGINES:
                if not self.items[eng]:
                    continue
                items = self.items[eng]

                def run(e, items=items):
                    for it in items:
                        if it[0] == "w":
                            e.wait_ge(it[1], it[2])
                        else:
                            ins = it[1](e)
                            if it[2] is not None:
                                ins.then_inc(it[2], it[3])

                getattr(block, eng)(run)


class Banks:

    def __init__(self, prog, banks):
        self.prog = prog
        self.banks = banks
        self.release = [None] * len(banks)
        self.held = [False] * len(banks)
        self.rr = 0

    def get(self, eng="tensor", hold=False):
        for _ in range(len(self.banks)):
            i = self.rr % len(self.banks)
            self.rr += 1
            if not self.held[i]:
                break
        else:
            raise RuntimeError("all banks held")
        if self.release[i] is not None:
            self.prog.wait(eng, self.release[i])
        self.held[i] = hold
        return i, self.banks[i]

    def set_release(self, i, tok):
        self.release[i] = tok
        self.held[i] = False


def _pretile_lhsT(w_in_out, n_m, n_kc):
    a = w_in_out.reshape(n_kc, 128, n_m, 128)
    return np.ascontiguousarray(a.transpose(2, 1, 0, 3))


def _host_prep(inputs):
    x = np.ascontiguousarray(np.asarray(inputs["L0_attn_resid_pre"], dtype=np.float32))
    Wq = np.asarray(inputs["L0_attn_Wq"], dtype=np.float32)
    Wk = np.asarray(inputs["L0_attn_Wk"], dtype=np.float32)
    Wv = np.asarray(inputs["L0_attn_Wv"], dtype=np.float32)
    Wo = np.asarray(inputs["L0_attn_Wo"], dtype=np.float32)
    gw = np.asarray(inputs["L0_ffn_gate_w"], dtype=np.float32)
    ebias = np.asarray(inputs["L0_ffn_expert_bias"], dtype=np.float32)
    sgw = np.asarray(inputs["L0_ffn_shared_gate_w"], dtype=np.float32)
    suw = np.asarray(inputs["L0_ffn_shared_up_w"], dtype=np.float32)
    sdw = np.asarray(inputs["L0_ffn_shared_down_w"], dtype=np.float32)
    egu = np.asarray(inputs["L0_ffn_experts_gate_up"], dtype=np.float32)
    edn = np.asarray(inputs["L0_ffn_experts_down"], dtype=np.float32)

    sh = {}
    sh["wq"] = _pretile_lhsT(np.ascontiguousarray(Wq.T), 16, 16)
    sh["wk"] = _pretile_lhsT(np.ascontiguousarray(Wk.T), 16, 16)
    sh["wv"] = np.ascontiguousarray(Wv.T.reshape(16, 128, 2048))
    sh["wo"] = _pretile_lhsT(np.ascontiguousarray(Wo.T), 16, 16)
    sh["sgw"] = _pretile_lhsT(sgw, 8, 16)
    sh["suw"] = _pretile_lhsT(suw, 8, 16)
    sh["sdw"] = _pretile_lhsT(sdw, 16, 8)
    sh["egu"] = np.ascontiguousarray(
        egu.reshape(E, 16, 128, 16, 128).transpose(0, 3, 2, 1, 4))
    sh["edn"] = np.ascontiguousarray(
        edn.reshape(E, 8, 128, 16, 128).transpose(0, 3, 2, 1, 4))
    sh["gwt"] = np.ascontiguousarray(gw.reshape(16, 128, 16).transpose(1, 0, 2))
    sh["ebias"] = np.ascontiguousarray(ebias.reshape(16, 1))

    oh = np.zeros((16, 2048), dtype=np.float32)
    for e in range(16):
        oh[e, e * 128:(e + 1) * 128] = 1.0
    sh["onehot"] = oh
    jj = np.arange(128)
    al = np.zeros((128, 512), dtype=np.float32)
    for h in range(16):
        for kc in range(32):
            al[:, h * 32 + kc] = np.float32(SLOPES[h]) * (128.0 * kc + jj)
    sh["alibi"] = al
    sh["ident"] = np.eye(128, dtype=np.float32)
    sh["ones_row"] = np.ones((1, 128), dtype=np.float32)
    ze = np.zeros((128, 2), dtype=np.float32); ze[:, 1] = 1e-5
    sh["zeps"] = ze
    sm = np.zeros((128, 2048), dtype=np.float32)
    sm[0, :] = np.repeat(np.array(SLOPES, dtype=np.float32), 128)
    sh["slopes_mat"] = sm.astype(bfnp)
    sh["ones_mat"] = np.ones((128, 128), dtype=bfnp)
    sh["ones_col"] = np.ones((128, 1), dtype=bfnp)

    in_maps = []
    ii = np.arange(128)
    for c in range(NCORE):
        b, par = c // 2, c % 2
        m = dict(sh)
        xb = x[b]
        m["x_fm"] = np.ascontiguousarray(xb.T).reshape(16, 128, 4096)
        xo = np.ascontiguousarray(xb.reshape(16, 2, 128, 2048)[:, par])
        m["x_own"] = np.ascontiguousarray(
            xo.transpose(2, 0, 1).reshape(16, 128, 2048))
        msk = np.zeros((128, 8, 4, 128), dtype=np.float32)
        for r in range(8):
            for v in range(4):
                Dv = 2 * v + par - r
                if Dv >= 1:
                    continue
                elif Dv == 0:
                    msk[:, r, v, :] = np.where(jj[:, None] <= ii[None, :], 0.0, NEG)
                else:
                    msk[:, r, v, :] = NEG
        m["masks"] = msk.reshape(128, 8, 512).astype(bfnp)
        nq = np.zeros((1, 2048), dtype=np.float32)
        for T in range(4):
            for v in range(4):
                u = 4 * T + v
                nq[0, T * 512 + v * 128:T * 512 + (v + 1) * 128] = -(
                    256.0 * u + 128.0 * par + ii)
        nq2 = np.zeros((128, 2048), dtype=np.float32)
        nq2[0] = nq[0]
        m["negq_bf"] = nq2.astype(bfnp)
        in_maps.append(m)
    return in_maps


def _assemble(results):
    out = np.empty((B, S, D), dtype=np.float32)
    for c in range(NCORE):
        b, par = c // 2, c % 2
        o = results[c]["out_fm"]
        arr = o.reshape(16, 128, 16, 128).transpose(2, 3, 0, 1)
        out[b].reshape(16, 2, 128, 2048)[:, par] = arr.reshape(16, 128, 2048)
    return out


def _build(dump=(), phases=("p1", "p2"), limit=None):
    nc = bass.Bass("TRN2", target_bir_lowering=False)
    st = ExitStack()
    p = Prog(nc, st)

    dr = {}
    for nm, shp, dt in [
        ("x_fm", [16, 128, 4096], F32), ("x_own", [16, 128, 2048], F32),
        ("wq", [16, 128, 16, 128], F32), ("wk", [16, 128, 16, 128], F32),
        ("wv", [16, 128, 2048], F32), ("wo", [16, 128, 16, 128], F32),
        ("sgw", [8, 128, 16, 128], F32), ("suw", [8, 128, 16, 128], F32),
        ("sdw", [16, 128, 8, 128], F32),
        ("egu", [E, 16, 128, 16, 128], F32), ("edn", [E, 16, 128, 8, 128], F32),
        ("gwt", [128, 16, 16], F32), ("ebias", [16, 1], F32),
        ("onehot", [16, 2048], F32),
        ("alibi", [128, 512], F32), ("ident", [128, 128], F32),
        ("ones_row", [1, 128], F32), ("ones_col", [128, 1], BF16),
        ("zeps", [128, 2], F32), ("slopes_mat", [128, 2048], BF16),
        ("ones_mat", [128, 128], BF16), ("negq_bf", [128, 2048], BF16),
        ("masks", [128, 8, 512], BF16),
    ]:
        dr[nm] = nc.dram_tensor(nm, shp, dt, kind="ExternalInput")
    dr["out_fm"] = nc.dram_tensor("out_fm", [16, 128, 2048], F32, kind="ExternalOutput")
    dr["k_s"] = nc.dram_tensor("k_s", [16, 8, 128, 512], BF16)
    dr["v_s"] = nc.dram_tensor("v_s", [16, 32, 128, 128], BF16)
    dr["g_s"] = nc.dram_tensor("g_s", [16, 128, 2048], BF16)
    dumps = {}
    for nm in dump:
        shp = {"k_s": [16, 8, 128, 512], "v_s": [16, 32, 128, 128],
               "g_s": [16, 128, 2048], "logits": [16, 2048],
               "xn": [16, 128, 2048], "q": [128, 16, 2048], "z": [128, 16, 2048],
               "comb": [16, 2048], "acc": [128, 16, 2048],
               "sgt": [128, 8, 16], "sbt": [128, 8, 16],
               "tops": [128, 8, 8]}[nm]
        dt = {"logits": F32, "acc": F32, "comb": F32, "sgt": F32, "sbt": F32, "tops": F32}.get(nm, BF16)
        dumps[nm] = nc.dram_tensor("dump_" + nm, shp, dt, kind="ExternalOutput")

    SB = {}

    def sb(name, shape, dt, stack=st):
        SB[name] = stack.enter_context(nc.sbuf_tensor(name, shape, dt))
        return SB[name]

    ebias_sb = sb("ebias_sb", [16, 1], F32)
    ident_sb = sb("ident_sb", [128, 128], F32)
    ones_row = sb("onesr_sb", [1, 128], F32)
    ones_col = sb("onesc_sb", [128, 1], BF16)
    zeps_sb = sb("zeps_sb", [128, 2], F32)
    onesm_sb = sb("onesm_sb", [128, 128], BF16)
    xnh = sb("xnh", [128, 2, 16, 1024], BF16)
    logit = sb("logitg", [16, 2, 1024], F32)
    qprev_pe = [None, None]

    PS = [st.enter_context(nc.psum_tensor(f"bank{i}", [128, 512], F32)) for i in range(8)]
    banks = Banks(p, PS)

    for nm, dst in [("ident", ident_sb), ("ones_row", ones_row),
                    ("ones_col", ones_col), ("ebias", ebias_sb),
                    ("zeps", zeps_sb), ("ones_mat", onesm_sb)]:
        p.dma(lambda e, a=dr[nm], d=dst: e.dma_start(
            out=d[tuple(slice(None) for _ in d.shape)], in_=a.ap()), "s_c")
    c_tok = p.now("s_c")

    p.barrier()

    class LNTile:

        def __init__(self, xt, slot, xb, stats, chs, loch, xn_dst_slices,
                     x_src_ap, router=False, xn_wait=(), loch_wait=()):
            self.xt, self.slot, self.xb = xt, slot, xb
            self.stats, self.chs, self.loch = stats, chs, loch
            self.xn_dst = xn_dst_slices
            self.x_src = x_src_ap
            self.router = router
            self.xn_wait = list(xn_wait)
            self.loch_wait = list(loch_wait)
            self.rt_tok = None

        def stage0(self, xb_wait):
            self.txb = p.gdma(lambda e: e.dma_start(
                out=self.xb[:, self.slot, :, :], in_=self.x_src),
                f"s_xb{self.slot}", waits=xb_wait)

        def stage0_sync(self, xt_wait, xb_wait):
            self.sync_mode = True
            self.tx = p.dma(lambda e: e.dma_start(
                out=self.xt[:, :, :], in_=self.x_src), "s_x", waits=xt_wait)
            self.xb_wait0 = list(xb_wait)

        def stage_a(self, xt_wait):
            xb, slot = self.xb, self.slot
            self.pe_prev = p.now("pe")
            self.v_prev = p.now("v")
            if getattr(self, "sync_mode", False):
                self.txb = p.op("scalar", lambda e: e.copy(
                    out=xb[:, slot, :, :], in_=self.xt[:, :, :]),
                    waits=[self.tx] + self.xb_wait0, inc="a")
            else:
                self.tx = p.dma(lambda e: e.dma_start(
                    out=self.xt[:, :, :], in_=self.x_src), "s_x", waits=xt_wait)
            self.bi_s, self.bk_s = banks.get(hold=True)
            ts_sum = None
            for kc in range(16):
                ts_sum = p.op("tensor", lambda e, bk=self.bk_s, kc=kc: e.matmul(
                    bk[0:1, :], ones_col[:, :], xb[:, slot, kc, :],
                    start=(kc == 0), stop=(kc == 15)),
                    waits=[self.txb, c_tok] if kc == 0 else [],
                    inc="pe" if kc == 15 else None)
            self.ts_sum = ts_sum
            self.tsq = p.op("scalar", lambda e: e.activation(
                out=xb[:, slot, :, :], in_=xb[:, slot, :, :], func=AF.Square,
                bias=zeps_sb[:, 0:1], scale=1.0),
                waits=[self.txb, ts_sum, self.pe_prev], inc="a")

        def stage_b(self):
            xb, slot, stats, chs = self.xb, self.slot, self.stats, self.chs
            xt, loch, router = self.xt, self.loch, self.router
            xn_dst_slices, tx = self.xn_dst, self.tx
            bi_q, bk_q = banks.get()
            ts_sq = None
            for kc in range(16):
                ts_sq = p.op("tensor", lambda e, bk=bk_q, kc=kc: e.matmul(
                    bk[0:1, :], ones_col[:, :], xb[:, slot, kc, :],
                    start=(kc == 0), stop=(kc == 15)),
                    waits=[self.tsq] if kc == 0 else [],
                    inc="pe" if kc == 15 else None)
            tm = p.op("scalar", lambda e, bk=self.bk_s: e.activation(
                out=stats[:, 0, :], in_=bk[0:1, :], func=AF.Copy,
                scale=1.0 / 2048.0, bias=0.0),
                waits=[self.ts_sum, self.v_prev], inc="a")
            te2 = p.op("scalar", lambda e, bk=bk_q: e.activation(
                out=stats[:, 1, :], in_=bk[0:1, :], func=AF.Copy,
                scale=1.0 / 2048.0, bias=0.0), waits=[ts_sq], inc="a")
            banks.set_release(self.bi_s, tm)
            banks.set_release(bi_q, te2)
            tv1 = p.op("vector", lambda e: e.tensor_tensor(
                out=stats[:, 2, :], in0=stats[:, 0, :], in1=stats[:, 0, :],
                op=ALU.mult), waits=[tm], inc="v")
            tv2 = p.op("vector", lambda e: e.tensor_tensor(
                out=stats[:, 2, :], in0=stats[:, 1, :], in1=stats[:, 2, :],
                op=ALU.subtract), waits=[te2, tv1], inc="v")
            tsq2 = p.op("scalar", lambda e: e.activation(
                out=stats[:, 1, :], in_=stats[:, 2, :], func=AF.Sqrt,
                bias=zeps_sb[0:1, 1:2], scale=1.0), waits=[tv2], inc="a")
            trr = p.op("vector", lambda e: e.reciprocal(
                out=stats[:, 1, :], in_=stats[:, 1, :]), waits=[tsq2], inc="v")
            bi_m, bk_m = banks.get(hold=True)
            tbm = p.op("tensor", lambda e, bk=bk_m: e.matmul(
                bk[:, :], ones_row[:, :], stats[:, 0, :], start=True, stop=True),
                waits=[tm], inc="pe")
            bi_r, bk_r = banks.get(hold=True)
            tbr = p.op("tensor", lambda e, bk=bk_r: e.matmul(
                bk[:, :], ones_row[:, :], stats[:, 1, :], start=True, stop=True),
                waits=[trr], inc="pe")
            if router:
                gw_f, rbx, logit_dst, gw_ld = self.router_ctx
                bi_rt, bk_rt = banks.get(hold=True)
                bi_rb, bk_rb = banks.get()
                trb = p.op("tensor", lambda e, bk=bk_rb: e.matmul(
                    bk[0:16, :], ones_row[0:1, 0:16], stats[:, 1, :],
                    start=True, stop=True), waits=[trr], inc="pe")
                trbe = p.op("scalar", lambda e, bk=bk_rb: e.copy(
                    out=rbx[:, :], in_=bk[0:16, :]), waits=[trb], inc="a")
                banks.set_release(bi_rb, trbe)
            xn_toks = []
            last_reader = None
            mm_toks = {}
            for kc in range(16):
                sl2 = kc % 2
                wk1 = ([mm_toks[kc - 2]] if kc >= 2 else
                       list(self.loch_wait)) if router else []
                td1 = p.op("vector", lambda e, kc=kc, sl2=sl2, bk=bk_m: e.tensor_tensor(
                    out=chs[:, sl2, :], in0=xt[:, kc, :], in1=bk[:, :],
                    op=ALU.subtract),
                    waits=[tbm, tx] + wk1 +
                    ([last_reader] if last_reader else []),
                    inc="v")
                td2 = p.op("vector", lambda e, kc=kc, sl2=sl2, bk=bk_r: e.tensor_tensor(
                    out=xn_dst_slices(kc), in0=chs[:, sl2, :], in1=bk[:, :],
                    op=ALU.mult), waits=[tbr, td1] + self.xn_wait, inc="v")
                xn_toks.append(td2)
                last_reader = td2
                if router:
                    mm_toks[kc] = p.op("tensor", lambda e, kc=kc, sl2=sl2, bk=bk_rt: e.matmul(
                        bk[0:16, :], gw_f[:, kc, :], chs[:, sl2, :],
                        start=(kc == 0), stop=(kc == 15)),
                        waits=[td1] + (gw_ld if kc == 0 else []), inc="pe")
            if router:
                self.rt_tok = p.op("vector", lambda e, bk=bk_rt: e.tensor_tensor(
                    out=logit_dst, in0=bk[0:16, :], in1=rbx[:, :],
                    op=ALU.mult), waits=[mm_toks[15], trbe], inc="v")
                banks.set_release(bi_rt, self.rt_tok)
                self.mm_last = [mm_toks[14], mm_toks[15]]
            else:
                self.rt_tok = None
            banks.set_release(bi_m, xn_toks[-1])
            banks.set_release(bi_r, p.now("v"))
            self.xn_toks = xn_toks
            self.xb_free = [self.tsq, ts_sq]
            self.xt_free = [p.now("v")]

    def emit_ln_tile(xt, slot, xb, stats, chs, loch, xn_dst_slices, x_src_ap,
                     xt_wait, xb_wait, router=None, xn_wait=()):
        t = LNTile(xt, slot, xb, stats, chs, loch, xn_dst_slices, x_src_ap,
                   router=router, xn_wait=xn_wait)
        t.stage0(xb_wait)
        t.stage_a(xt_wait)
        t.stage_b()
        return t.xt_free, t.xb_free, t.xn_toks, t.rt_tok

    ln2a = {}

    def make_2a(hh, xt_, xb_, stats_, chs_, slots, router_ctx):
        tiles = []
        for tl in range(2):
            src = dr["x_own"].ap()[:, :, hh * 1024 + tl * 512:
                                   hh * 1024 + (tl + 1) * 512] \
                .rearrange("c f s -> f c s")
            gw_f, rbx, gw_ld = router_ctx
            tiles.append(LNTile(
                xt_, slots[tl], xb_, stats_, chs_, None,
                lambda kc, hh=hh, tl=tl: xnh[:, hh, kc,
                                             tl * 512:(tl + 1) * 512],
                src, router=True,
                xn_wait=[qprev_pe[hh]] if qprev_pe[hh] else []))
            tiles[tl].router_ctx = (
                gw_f, rbx,
                logit[:, hh, tl * 512:(tl + 1) * 512], gw_ld)
        ln2a[hh] = {"tiles": tiles, "lg": [], "xn": []}
        return tiles

    def drive_2a_b(hh, tl):
        t = ln2a[hh]["tiles"][tl]
        if tl == 1:
            t.loch_wait = ln2a[hh]["tiles"][0].mm_last
        t.stage_b()
        ln2a[hh]["xn"].append(t.xn_toks[-1])
        ln2a[hh]["lg"].append(t.rt_tok)

    if "p1" in phases:
        with ExitStack() as ph:
            xt = sb("xt", [128, 16, 512], F32, ph)
            xb = sb("xb", [128, 2, 16, 512], BF16, ph)
            stats = sb("stats", [1, 3, 512], F32, ph)
            chs = sb("chs", [128, 2, 512], F32, ph)
            gwf = sb("gwf", [128, 16, 16], F32, ph)
            rbx1 = sb("rbx1", [16, 512], F32, ph)
            tgwl = p.dma(lambda e: e.dma_start(
                out=gwf[:, :, :], in_=dr["gwt"].ap()), "s_c")
            gw_tok = tgwl
            wslab_b = sb("wslab_b", [128, 2, 16, 128], BF16, ph)
            wvb = sb("wvb", [128, 4, 512], BF16, ph)
            kev = sb("kev", [128, 2, 512], BF16, ph)
            vev = sb("vev", [128, 8, 512], BF16, ph)

            xb_free = [[], []]
            kb_free = [None, None]
            vb_free = [None, None, None, None]
            kev_free = [None, None]
            vev_free = [None] * 8
            xt_state = {"free": []}

            def prep_quarter(qt):
                qs = qt % 2
                objs = []
                for tl in range(2):
                    stg = qt * 2 + tl
                    xsrc = dr["x_fm"].ap()[:, :, stg * 512:(stg + 1) * 512] \
                        .rearrange("c f s -> f c s")
                    objs.append(LNTile(
                        xt, tl, xb, stats, chs, None,
                        lambda kc, tl=tl, qs=qs: xnh[:, qs, kc,
                                                     tl * 512:(tl + 1) * 512],
                        xsrc, router=None,
                        xn_wait=[qprev_pe[qs]] if qprev_pe[qs] else []))
                return objs

            def _ln_a(t):
                t.stage_a(xt_state["free"])

            def _ln_b(t):
                t.stage_b()
                xt_state["free"] = t.xt_free

            wk_pre = {}

            def issue_wk(qt, m):
                wk_pre[(qt, m)] = p.gdma(lambda e, m=m: e.dma_start(
                    out=wslab_b[:, m % 2, :, :], in_=dr["wk"].ap()[m]),
                    f"s_wk{m % 2}",
                    waits=[kb_free[m % 2]] if kb_free[m % 2] else [])

            def kv_quarter(qt, xn_tok, drive, pre=None, post=None, mid=None):
                qs = qt % 2
                for m in (0, 1):
                    if (qt, m) not in wk_pre:
                        issue_wk(qt, m)
                if pre is not None:
                    pre()
                for m in range(16):
                    if m in drive:
                        drive[m]()
                    sl = m % 2
                    tc = wk_pre.pop((qt, m), None)
                    if tc is None:
                        tc = p.gdma(lambda e, m=m, sl=sl: e.dma_start(
                            out=wslab_b[:, sl, :, :], in_=dr["wk"].ap()[m]),
                            f"s_wk{sl}",
                            waits=[kb_free[sl]] if kb_free[sl] else [])
                    grp = [banks.get(), banks.get()]
                    tg = None
                    ktok = [None, None]
                    for kc in range(16):
                        for tt in range(2):
                            tg = p.op("tensor", lambda e, bk=grp[tt][1], sl=sl, kc=kc, tt=tt, qs=qs: e.matmul(
                                bk[:, :], wslab_b[:, sl, kc, :],
                                xnh[:, qs, kc, tt * 512:(tt + 1) * 512],
                                start=(kc == 0), stop=(kc == 15)),
                                waits=[tc, xn_tok[tt]] if kc == 0 else [],
                                inc="pe" if kc == 15 else None)
                            if kc == 15:
                                ktok[tt] = tg
                    kb_free[sl] = tg
                    for tt in range(2):
                        bi, bk = grp[tt]
                        ev = kev[:, tt, :]
                        tev = p.op("scalar", lambda e, bk=bk, ev=ev: e.copy(
                            out=ev, in_=bk[:, :]),
                            waits=[ktok[tt]] + ([kev_free[tt]] if kev_free[tt] else []),
                            inc="a")
                        banks.set_release(bi, tev)
                        seg = qt * 2 + tt
                        tdm = p.dma(lambda e, m=m, seg=seg, ev=ev: e.dma_start(
                            out=bass.AP(dr["k_s"], m * 524288 + seg * 65536,
                                        [[512, 128], [1, 512]]),
                            in_=ev), f"s_spk{tt}", waits=[tev])
                        kev_free[tt] = tdm
                for ct in range(4):
                    vgrp = []
                    for tb in range(8):
                        vgrp.append(banks.get())
                    tg = None
                    vtok = [None] * 8
                    for kc in range(16):
                        sl = kc % 4
                        tc = p.gdma(lambda e, kc=kc, ct=ct, sl=sl: e.dma_start(
                            out=wvb[:, sl, :],
                            in_=bass.AP(dr["wv"], kc * 128 * 2048 + ct * 512,
                                        [[2048, 128], [1, 512]])), f"s_wv{sl}",
                            waits=[vb_free[sl]] if vb_free[sl] else [])
                        for tb in range(8):
                            tg = p.op("tensor", lambda e, bk=vgrp[tb][1], kc=kc, tb=tb, sl=sl, qs=qs: e.matmul(
                                bk[:, :], xnh[:, qs, kc, tb * 128:(tb + 1) * 128],
                                wvb[:, sl, :], start=(kc == 0), stop=(kc == 15)),
                                waits=([tc, xn_tok[0]] if tb == 0 else
                                       ([xn_tok[1]] if (tb == 4 and kc == 0)
                                        else [])),
                                inc="pe" if (tb == 7 or kc == 15) else None)
                            if kc == 15:
                                vtok[tb] = tg
                        vb_free[sl] = tg
                    for tb in range(8):
                        bi, bk = vgrp[tb]
                        ev = vev[:, tb, :]
                        tev = p.op("scalar", lambda e, bk=bk, ev=ev: e.copy(
                            out=ev, in_=bk[:, :]),
                            waits=[vtok[tb]] + ([vev_free[tb]] if vev_free[tb] else []),
                            inc="a")
                        banks.set_release(bi, tev)
                        kb = qt * 8 + tb
                        tdm = p.dma(lambda e, kb=kb, ct=ct, ev=ev: e.dma_start(
                            out=bass.AP(dr["v_s"], 4 * ct * 524288 + kb * 16384,
                                        [[128, 128], [524288, 4], [1, 128]]),
                            in_=ev.rearrange("kk (hh dh) -> kk hh dh", hh=4)),
                            f"s_spv{tb}", waits=[tev])
                        vev_free[tb] = tdm
                    if ct == 0 and mid is not None:
                        mid()
                if post is not None:
                    post()
                qprev_pe[qs] = p.now("pe")

            cur = prep_quarter(0)
            for tl in range(2):
                cur[tl].stage0(xb_free[tl])
                _ln_a(cur[tl])
                _ln_b(cur[tl])
            xn_tok = (cur[0].xn_toks[-1], cur[1].xn_toks[-1])
            for qt in range(4):
                drive, pre, nxt, post, mid = {}, None, None, None, None
                if qt + 1 < 4:
                    nxt = prep_quarter(qt + 1)

                    def pre(nxt=nxt, cur=cur):
                        nxt[0].stage0(cur[0].xb_free)
                        nxt[1].stage0(cur[1].xb_free)
                    drive = {4: (lambda t=nxt[0]: _ln_a(t)),
                             8: (lambda t=nxt[0]: _ln_b(t)),
                             11: (lambda t=nxt[1]: _ln_a(t)),
                             14: (lambda t=nxt[1]: _ln_b(t))}
                    mid = (lambda qt=qt: (issue_wk(qt + 1, 0),
                                          issue_wk(qt + 1, 1)))
                else:
                    t2a = make_2a(0, xt, xb, stats, chs, (0, 1),
                                  (gwf, rbx1, [gw_tok]))

                    def pre(t2a=t2a, cur=cur):
                        t2a[0].stage0(cur[0].xb_free)
                        t2a[1].stage0(cur[1].xb_free)

                    def a2(tl):
                        ln2a[0]["tiles"][tl].stage_a(xt_state["free"])

                    def b2(tl):
                        drive_2a_b(0, tl)
                        xt_state["free"] = ln2a[0]["tiles"][tl].xt_free
                    drive = {4: (lambda: a2(0)), 8: (lambda: b2(0)),
                             11: (lambda: a2(1)), 14: (lambda: b2(1))}
                kv_quarter(qt, xn_tok, drive, pre, post, mid)
                if nxt is not None:
                    cur = nxt
                    xn_tok = (nxt[0].xn_toks[-1], nxt[1].xn_toks[-1])
            p.barrier()
        if "k_s" in dumps:
            p.dma(lambda e: e.dma_start(out=dumps["k_s"].ap(), in_=dr["k_s"].ap()), "s_d")
        if "v_s" in dumps:
            p.dma(lambda e: e.dma_start(out=dumps["v_s"].ap(), in_=dr["v_s"].ap()), "s_d")

    if "p2" in phases:
        def emit_phase2(hh):
            with ExitStack() as ph:
                sb(f"shim_ph{hh}", [128, 8 + 64 * hh], F32, ph)
                wgu_b = sb(f"wgub{hh}", [128, 2, 16, 128], BF16, ph)

                xn_tok = tuple(ln2a[hh]["xn"])
                lg_tok = ln2a[hh]["lg"]
                if "xn" in dumps:
                    for kc in range(16):
                        p.dma(lambda e, hh=hh, kc=kc: e.dma_start(
                            out=dumps["xn"].ap()[kc, :, hh * 1024:(hh + 1) * 1024],
                            in_=xnh[:, hh, kc, :]), "s_d", waits=list(xn_tok))
                if "logits" in dumps:
                    p.dma(lambda e, hh=hh: e.dma_start(
                        out=dumps["logits"].ap()[:, hh * 1024:(hh + 1) * 1024],
                        in_=logit[:, hh, :]), "s_d", waits=lg_tok)

                psB = ExitStack()
                sb(f"shim_psB{hh}", [128, 32], F32, psB)
                acc = sb(f"acc{hh}", [128, 16, 1024], F32, psB)
                gsil = sb(f"gsil{hh}", [128, 2, 1024], BF16, psB)
                comb_sb = sb(f"comb{hh}", [16, 1024], F32, psB)
                cb_sb = sb(f"cb{hh}", [128, 1024], F32, psB)
                onehot_sb = sb(f"onehot{hh}", [16, 2048], F32, psB)
                p.dma(lambda e: e.dma_start(
                    out=onehot_sb[:, :], in_=dr["onehot"].ap()), f"s_oh{hh}")
                oh_tok = p.now(f"s_oh{hh}")

                ps3 = ExitStack()
                sb(f"shim_ps3{hh}", [128, 24], F32, ps3)
                sg_sb = sb(f"sg{hh}", [16, 1024], F32, ps3)
                sbb_sb = sb(f"sbb{hh}", [16, 1024], F32, ps3)
                sgT = sb(f"sgT{hh}", [128, 8, 16], F32, ps3)
                sbT = sb(f"sbT{hh}", [128, 8, 16], F32, ps3)
                tops = sb(f"tops{hh}", [128, 8, 8], F32, ps3)
                tscr = sb(f"tscr{hh}", [128, 5, 16], F32, ps3)

                tsg = p.op("scalar", lambda e: e.activation(
                    out=sg_sb[:, :], in_=logit[:, hh, :], func=AF.Sigmoid,
                    bias=zeps_sb[0:16, 0:1],
                    scale=1.0), waits=lg_tok, inc="a")
                tsb = p.op("vector", lambda e: e.tensor_scalar(
                    out=sbb_sb[:, :], in0=sg_sb[:, :], scalar1=ebias_sb[:, :],
                    scalar2=None, op0=ALU.add), waits=[tsg, c_tok], inc="v")
                for blk in range(8):
                    bi1, bk1 = banks.get()
                    tt1 = p.op("tensor", lambda e, bk=bk1, blk=blk: e.transpose(
                        bk[0:128, 0:16], sg_sb[:, blk * 128:(blk + 1) * 128],
                        ident_sb[0:16, 0:16]), waits=[tsg], inc="pe")
                    te1 = p.op("scalar", lambda e, bk=bk1, blk=blk: e.copy(
                        out=sgT[:, blk, :], in_=bk[0:128, 0:16]), waits=[tt1], inc="a")
                    banks.set_release(bi1, te1)
                    bi2, bk2 = banks.get()
                    tt2 = p.op("tensor", lambda e, bk=bk2, blk=blk: e.transpose(
                        bk[0:128, 0:16], sbb_sb[:, blk * 128:(blk + 1) * 128],
                        ident_sb[0:16, 0:16]), waits=[tsb], inc="pe")
                    te2_ = p.op("scalar", lambda e, bk=bk2, blk=blk: e.copy(
                        out=sbT[:, blk, :], in_=bk[0:128, 0:16]), waits=[tt2], inc="a")
                    banks.set_release(bi2, te2_)
                    m1 = tops[:, blk, 0:1]; m2 = tops[:, blk, 1:2]
                    s1 = tops[:, blk, 2:3]; s2 = tops[:, blk, 3:4]
                    rec = tops[:, blk, 4:5]; w1 = tops[:, blk, 5:6]; w2 = tops[:, blk, 6:7]
                    eq1 = tscr[:, 0, :]; mskd = tscr[:, 1, :]; tmul = tscr[:, 2, :]
                    eq2 = tscr[:, 3, :]
                    x_ = sbT[:, blk, :]; g_ = sgT[:, blk, :]
                    ops = [
                        lambda e, x_=x_, m1=m1: e.reduce_max(m1, x_, axis=AX.X),
                        lambda e, x_=x_, m1=m1, eq1=eq1: e.tensor_scalar(
                            out=eq1, in0=x_, scalar1=m1, scalar2=None, op0=ALU.is_equal),
                        lambda e, x_=x_, eq1=eq1, mskd=mskd: e.scalar_tensor_tensor(
                            out=mskd, in0=eq1, scalar=NEG, in1=x_, op0=ALU.mult,
                            op1=ALU.add),
                        lambda e, m2=m2, mskd=mskd: e.reduce_max(m2, mskd, axis=AX.X),
                        lambda e, g_=g_, eq1=eq1, tmul=tmul: e.tensor_tensor(
                            out=tmul, in0=g_, in1=eq1, op=ALU.mult),
                        lambda e, s1=s1, tmul=tmul: e.reduce_sum(s1, tmul, axis=AX.X),
                        lambda e, m2=m2, mskd=mskd, eq2=eq2: e.tensor_scalar(
                            out=eq2, in0=mskd, scalar1=m2, scalar2=None,
                            op0=ALU.is_equal),
                        lambda e, g_=g_, eq2=eq2, tmul=tmul: e.tensor_tensor(
                            out=tmul, in0=g_, in1=eq2, op=ALU.mult),
                        lambda e, s2=s2, tmul=tmul: e.reduce_sum(s2, tmul, axis=AX.X),
                        lambda e, rec=rec, s1=s1, s2=s2: e.tensor_tensor(
                            out=rec, in0=s1, in1=s2, op=ALU.add),
                        lambda e, rec=rec: e.tensor_scalar(
                            out=rec, in0=rec, scalar1=1e-20, scalar2=None, op0=ALU.add),
                        lambda e, rec=rec: e.reciprocal(out=rec, in_=rec),
                        lambda e, w1=w1, s1=s1, rec=rec: e.tensor_tensor(
                            out=w1, in0=s1, in1=rec, op=ALU.mult),
                        lambda e, w2=w2, s2=s2, rec=rec: e.tensor_tensor(
                            out=w2, in0=s2, in1=rec, op=ALU.mult),
                        lambda e, eq1=eq1, w1=w1: e.tensor_scalar(
                            out=eq1, in0=eq1, scalar1=w1, scalar2=None, op0=ALU.mult),
                        lambda e, eq1=eq1, eq2=eq2, w2=w2: e.scalar_tensor_tensor(
                            out=eq1, in0=eq2, scalar=w2, in1=eq1, op0=ALU.mult,
                            op1=ALU.add),
                    ]
                    tv = (te1[0], te1[1])
                    p.wait("vector", te2_)
                    for fn in ops:
                        tv = p.op("vector", fn, waits=[tv], inc="v")
                    bi3, bk3 = banks.get()
                    tt3 = p.op("tensor", lambda e, bk=bk3, eq1=eq1: e.transpose(
                        bk[0:16, 0:128], eq1, ident_sb[:, :]), waits=[tv], inc="pe")
                    te3 = p.op("scalar", lambda e, bk=bk3, blk=blk: e.copy(
                        out=comb_sb[:, blk * 128:(blk + 1) * 128], in_=bk[0:16, 0:128]),
                        waits=[tt3], inc="a")
                    banks.set_release(bi3, te3)
                comb_tok = p.now("a")
                if hh == 0:
                    for dn, dtsr in [("sgt", sgT), ("sbt", sbT), ("tops", tops)]:
                        if dn in dumps:
                            p.dma(lambda e, dn=dn, dtsr=dtsr: e.dma_start(
                                out=dumps[dn].ap(), in_=dtsr[:, :, :]), "s_d",
                                waits=[comb_tok, p.now("v")])
                ps3.close()
                if "comb" in dumps:
                    p.dma(lambda e, hh=hh: e.dma_start(
                        out=dumps["comb"].ap()[:, hh * 1024:(hh + 1) * 1024],
                        in_=comb_sb[:, :]), "s_d", waits=[comb_tok])

                if limit == "comb":
                    p.barrier()
                    return
                hprime = sb(f"hp{hh}", [128, 8, 1024], BF16, psB)
                wdn_b = sb(f"wdnb{hh}", [128, 2, 8, 128], BF16, psB)
                slabs = {"gb": [None, None], "db": [None, None]}
                acc_last = {}
                gs_free = [None, None]

                def gu_down_pass(gu_aps, dn_aps, cbc, init_acc, init_tok,
                                 tail_cb=None):
                    for m in range(16):
                        sl = m % 2
                        tc = p.gdma(lambda e, ap=gu_aps[m], sl=sl: e.dma_start(
                            out=wgu_b[:, sl, :, :], in_=ap), f"s_wg{sl}",
                            waits=[slabs["gb"][sl]] if slabs["gb"][sl] else [])
                        tg = None
                        grp = [banks.get(), banks.get()]
                        for kc in range(16):
                            for tt in range(2):
                                tg = p.op("tensor", lambda e, bk=grp[tt][1], sl=sl, kc=kc, tt=tt: e.matmul(
                                    bk[:, :], wgu_b[:, sl, kc, :],
                                    xnh[:, hh, kc, tt * 512:(tt + 1) * 512],
                                    start=(kc == 0), stop=(kc == 15)),
                                    waits=[tc, xn_tok[tt]] if kc == 0 else [],
                                    inc="pe" if (kc == 15 and tt == 1) else None)
                        pr, gsl = m // 2, (m // 2) % 2
                        for tt in range(2):
                            bi, bk = grp[tt]
                            if m % 2 == 0:
                                tev = p.op("scalar", lambda e, bk=bk, gsl=gsl, tt=tt: e.activation(
                                    out=gsil[:, gsl, tt * 512:(tt + 1) * 512],
                                    in_=bk[:, :], func=AF.Silu, bias=zeps_sb[:, 0:1], scale=1.0),
                                    waits=[tg] +
                                    ([gs_free[gsl]] if (tt == 0 and gs_free[gsl]) else []),
                                    inc="a")
                                banks.set_release(bi, tev)
                            else:
                                tev = p.op("vector", lambda e, bk=bk, pr=pr, gsl=gsl, tt=tt: e.tensor_tensor(
                                    out=hprime[:, pr, tt * 512:(tt + 1) * 512],
                                    in0=bk[:, :],
                                    in1=gsil[:, gsl, tt * 512:(tt + 1) * 512],
                                    op=ALU.mult), waits=[tg, p.now("a")], inc="v")
                                if tt == 1:
                                    gs_free[gsl] = tev
                                if cbc is not None:
                                    tev = p.op("vector", lambda e, pr=pr, tt=tt: e.tensor_tensor(
                                        out=hprime[:, pr, tt * 512:(tt + 1) * 512],
                                        in0=hprime[:, pr, tt * 512:(tt + 1) * 512],
                                        in1=cb_sb[:, tt * 512:(tt + 1) * 512],
                                        op=ALU.mult), waits=[cbc, tev], inc="v")
                                banks.set_release(bi, tev)
                        slabs["gb"][sl] = tg
                    hp_tok = p.now("v")
                    gs_tok = p.now("a")
                    for m in range(16):
                        sl = m % 2
                        tc = p.gdma(lambda e, ap=dn_aps[m], sl=sl: e.dma_start(
                            out=wdn_b[:, sl, :, :], in_=ap), f"s_wd{sl}",
                            waits=[slabs["db"][sl]] if slabs["db"][sl] else [])
                        tg = None
                        grp = [banks.get(), banks.get()]
                        for kc in range(8):
                            for tt in range(2):
                                tg = p.op("tensor", lambda e, bk=grp[tt][1], sl=sl, kc=kc, tt=tt: e.matmul(
                                    bk[:, :], wdn_b[:, sl, kc, :],
                                    hprime[:, kc, tt * 512:(tt + 1) * 512],
                                    start=(kc == 0), stop=(kc == 7)),
                                    waits=[tc, hp_tok, gs_tok] if kc == 0 else [],
                                    inc="pe" if (kc == 7 and tt == 1) else None)
                        add_toks = []
                        for tt in range(2):
                            bi, bk = grp[tt]
                            prev_acc = acc_last.get((m, tt))
                            if init_acc:
                                tev = p.op("scalar", lambda e, bk=bk, m=m, tt=tt: e.copy(
                                    out=acc[:, m, tt * 512:(tt + 1) * 512], in_=bk[:, :]),
                                    waits=[tg], inc="a")
                            else:
                                tev = p.op("vector", lambda e, bk=bk, m=m, tt=tt: e.tensor_tensor(
                                    out=acc[:, m, tt * 512:(tt + 1) * 512],
                                    in0=acc[:, m, tt * 512:(tt + 1) * 512],
                                    in1=bk[:, :], op=ALU.add),
                                    waits=[tg, init_tok] +
                                    ([prev_acc] if prev_acc else []), inc="v")
                            acc_last[(m, tt)] = tev
                            add_toks.append(tev)
                            banks.set_release(bi, tev)
                        if tail_cb is not None:
                            tail_cb(m, add_toks)
                        slabs["db"][sl] = tg
                    return p.now("a"), p.now("v")

                sh_gu = []
                for pr in range(8):
                    sh_gu += [dr["sgw"].ap()[pr], dr["suw"].ap()[pr]]
                sh_dn = [dr["sdw"].ap()[m] for m in range(16)]
                init_tok, _ = gu_down_pass(sh_gu, sh_dn, None, True, None)

                g_toks = []
                gslot_free = [None, None]

                def g_tail(m, add_toks, hh=hh):
                    sl = m % 2
                    tgs = p.op("scalar", lambda e, m=m, sl=sl: e.activation(
                        out=gsil[:, sl, :], in_=acc[:, m, :], func=AF.Silu,
                        bias=zeps_sb[:, 0:1], scale=1.0),
                        waits=add_toks +
                        ([gslot_free[sl]] if gslot_free[sl] else []), inc="a")
                    tdm = p.dma(lambda e, m=m, hh=hh, sl=sl: e.dma_start(
                        out=bass.AP(dr["g_s"], m * 128 * 2048 + hh * 1024,
                                    [[2048, 128], [1, 1024]]),
                        in_=gsil[:, sl, :]), f"s_g{sl}", waits=[tgs])
                    gslot_free[sl] = tdm
                    g_toks.append(tdm)

                for ex in range(E):
                    cb_tok = None
                    for tt in range(2):
                        bi, bk = banks.get()
                        tmm = p.op("tensor", lambda e, bk=bk, ex=ex, tt=tt: e.matmul(
                            bk[:, :], onehot_sb[:, ex * 128:(ex + 1) * 128],
                            comb_sb[:, tt * 512:(tt + 1) * 512],
                            start=True, stop=True),
                            waits=[comb_tok, oh_tok], inc="pe")
                        cb_tok = p.op("scalar", lambda e, bk=bk, tt=tt: e.copy(
                            out=cb_sb[:, tt * 512:(tt + 1) * 512], in_=bk[:, :]),
                            waits=[tmm, p.now("v")], inc="a")
                        banks.set_release(bi, cb_tok)
                    eg = []
                    for pr in range(8):
                        eg += [dr["egu"].ap()[ex, pr], dr["egu"].ap()[ex, 8 + pr]]
                    ed = [dr["edn"].ap()[ex, m] for m in range(16)]
                    _, _ = gu_down_pass(eg, ed, cb_tok, False, init_tok,
                                        tail_cb=g_tail if ex == E - 1 else None)
                acc_tok_a, acc_tok_v = p.now("a"), p.now("v")
                if "acc" in dumps:
                    p.dma(lambda e, hh=hh: e.dma_start(
                        out=dumps["acc"].ap()[:, :, hh * 1024:(hh + 1) * 1024],
                        in_=acc[:, :, :]), "s_d", waits=[acc_tok_a, acc_tok_v])

                p.barrier(exclude=("gpsimd",))
                psB.close()
                if limit == "acc":
                    return

                z_sb = sb(f"z{hh}", [128, 16, 1024], BF16, ph)
                psC = ExitStack()
                sb(f"shim_psC{hh}", [128, 40], F32, psC)
                q_sb = sb(f"q{hh}", [128, 16, 1024], BF16, psC)
                for m in range(16):
                    sl = m % 2
                    tc = p.gdma(lambda e, m=m, sl=sl: e.dma_start(
                        out=wgu_b[:, sl, :, :], in_=dr["wq"].ap()[m]), f"s_wg{sl}",
                        waits=[slabs["gb"][sl]] if slabs["gb"][sl] else [])
                    tg = None
                    grp = [banks.get(), banks.get()]
                    for kc in range(16):
                        for tt in range(2):
                            tg = p.op("tensor", lambda e, bk=grp[tt][1], sl=sl, kc=kc, tt=tt: e.matmul(
                                bk[:, :], wgu_b[:, sl, kc, :],
                                xnh[:, hh, kc, tt * 512:(tt + 1) * 512],
                                start=(kc == 0), stop=(kc == 15)),
                                waits=[tc, xn_tok[tt]] if kc == 0 else [],
                                inc="pe" if (kc == 15 and tt == 1) else None)
                    for tt in range(2):
                        bi, bk = grp[tt]
                        tev = p.op("scalar", lambda e, bk=bk, m=m, tt=tt: e.activation(
                            out=q_sb[:, m, tt * 512:(tt + 1) * 512], in_=bk[:, :],
                            func=AF.Copy, bias=0.0, scale=QSCALE), waits=[tg], inc="a")
                        banks.set_release(bi, tev)
                    slabs["gb"][sl] = tg
                q_tok = p.now("a")
                if "q" in dumps:
                    p.dma(lambda e, hh=hh: e.dma_start(
                        out=dumps["q"].ap()[:, :, hh * 1024:(hh + 1) * 1024],
                        in_=q_sb[:, :, :]), "s_d", waits=[q_tok])
                if limit == "q":
                    p.barrier()
                    psC.close()
                    return

                k_sb = sb(f"k{hh}", [128, 2, 8, 512], BF16, psC)
                v_sb = sb(f"v{hh}", [128, 2, 32, 128], BF16, psC)
                sc2 = sb(f"sc2{hh}", [128, 2, 512], F32, psC)
                pbf = sb(f"pbf{hh}", [128, 3, 512], BF16, psC)
                rbb = sb(f"rbb{hh}", [128, 512], F32, psC)
                alibi_sb = sb(f"alibi{hh}", [128, 512], F32, psC)
                masks_sb = sb(f"masks{hh}", [128, 8, 512], BF16, psC)
                slbc_sb = sb(f"slbc{hh}", [128, 2048], BF16, psC)
                negqbf_sb = sb(f"negqbf{hh}", [128, 2048], BF16, psC)
                for nm, dst in [("alibi", alibi_sb), ("masks", masks_sb),
                                ("slopes_mat", slbc_sb), ("negq_bf", negqbf_sb)]:
                    p.dma(lambda e, a=dr[nm], d=dst: e.dma_start(
                        out=d[tuple(slice(None) for _ in d.shape)], in_=a.ap()),
                        f"s_ac{hh}")
                atc = p.now(f"s_ac{hh}")

                kv_free = [None, None]
                sc2_free = [None, None]
                pbf_free = [None, None, None]
                sp_toks = [p.now(f"s_spk{i}") for i in range(2)] + \
                    [p.now(f"s_spv{i}") for i in range(8)]
                for h in range(16):
                    sl = h % 2
                    w_ = ([kv_free[sl]] if kv_free[sl] else []) + sp_toks
                    tkh = p.dma(lambda e, h=h, sl=sl: e.dma_start(
                        out=k_sb[:, sl, :, :],
                        in_=bass.AP(dr["k_s"], h * 524288,
                                    [[512, 128], [65536, 8], [1, 512]])),
                        f"s_k{sl}", waits=w_)
                    for vq in range(4):
                        tvh = p.dma(lambda e, h=h, sl=sl, vq=vq: e.dma_start(
                            out=v_sb[:, sl, vq * 8:(vq + 1) * 8, :],
                            in_=bass.AP(dr["v_s"], h * 524288 + vq * 8 * 16384,
                                        [[128, 128], [16384, 8], [1, 128]])),
                            f"s_v{sl}", waits=w_)
                    last_tz = None
                    for Tl in range(2):
                        T = 2 * hh + Tl
                        nkc = 8 * (T + 1)
                        bi_z, bk_z = banks.get(hold=True)
                        bi_sm, bk_sm = banks.get(hold=True)
                        DEPTH = 4
                        sgrp = {}

                        def issue_S(kc, h=h, sl=sl, Tl=Tl, T=T):
                            bi_s, bk_s = banks.get(hold=True)
                            p.op("tensor", lambda e, bk=bk_s, h=h, sl=sl, kc=kc, Tl=Tl: e.matmul(
                                bk[:, :],
                                k_sb[:, sl, kc // 4, (kc % 4) * 128:(kc % 4 + 1) * 128],
                                q_sb[:, h, Tl * 512:(Tl + 1) * 512],
                                start=True, stop=False),
                                waits=[tkh, q_tok])
                            t2 = p.op("tensor", lambda e, bk=bk_s, h=h, T=T: e.matmul(
                                bk[:, :], slbc_sb[:, h * 128:(h + 1) * 128],
                                negqbf_sb[:, T * 512:(T + 1) * 512],
                                start=False, stop=True), waits=[atc], inc="pe")
                            sgrp[kc] = (bi_s, bk_s, t2)

                        for j in range(min(DEPTH, nkc)):
                            issue_S(j)
                        tav = None
                        for kc in range(nkc):
                            bi_s, bk_s, t2 = sgrp.pop(kc)
                            si = kc % 2
                            tcl = p.op("vector", lambda e, bk=bk_s, si=si: e.tensor_scalar(
                                out=sc2[:, si, :], in0=bk[:, :], scalar1=1024.0,
                                scalar2=-30000.0, op0=ALU.min, op1=ALU.max),
                                waits=[t2] + ([sc2_free[si]] if sc2_free[si] else []),
                                inc="v")
                            banks.set_release(bi_s, tcl)
                            tb2 = tcl
                            if kc >= 8 * T:
                                tb2 = p.op("vector", lambda e, si=si, r=kc - 8 * T: e.tensor_tensor(
                                    out=sc2[:, si, :], in0=sc2[:, si, :],
                                    in1=masks_sb[:, r, :], op=ALU.add),
                                    waits=[tb2, atc], inc="v")
                            pi = kc % 3
                            texp = p.op("scalar", lambda e, si=si, pi=pi, h=h, kc=kc: e.activation(
                                out=pbf[:, pi, :], in_=sc2[:, si, :], func=AF.Exp,
                                bias=alibi_sb[:, h * 32 + kc:h * 32 + kc + 1],
                                scale=1.0),
                                waits=[tb2, atc] +
                                ([pbf_free[pi]] if pbf_free[pi] else []),
                                inc="a")
                            sc2_free[si] = texp
                            p.op("tensor", lambda e, bk=bk_sm, pi=pi, kc=kc, nkc=nkc: e.matmul(
                                bk[:, :], onesm_sb[:, :], pbf[:, pi, :],
                                start=(kc == 0), stop=(kc == nkc - 1)), waits=[texp])
                            tav = p.op("tensor", lambda e, bk=bk_z, pi=pi, sl=sl, kc=kc, nkc=nkc: e.matmul(
                                bk[:, :], v_sb[:, sl, kc, :], pbf[:, pi, :],
                                start=(kc == 0), stop=(kc == nkc - 1)),
                                waits=[tvh], inc="pe")
                            pbf_free[pi] = tav
                            if kc + DEPTH < nkc:
                                issue_S(kc + DEPTH)
                        trc = p.op("vector", lambda e, bk=bk_sm: e.reciprocal(
                            out=rbb[:, :], in_=bk[:, :]), waits=[tav], inc="v")
                        banks.set_release(bi_sm, trc)
                        tz = p.op("vector", lambda e, bk=bk_z, h=h, Tl=Tl: e.tensor_tensor(
                            out=z_sb[:, h, Tl * 512:(Tl + 1) * 512], in0=bk[:, :],
                            in1=rbb[:, :], op=ALU.mult), waits=[trc], inc="v")
                        banks.set_release(bi_z, tz)
                        last_tz = tz
                    kv_free[sl] = last_tz
                z_tok = p.now("v")
                if "z" in dumps:
                    p.dma(lambda e, hh=hh: e.dma_start(
                        out=dumps["z"].ap()[:, :, hh * 1024:(hh + 1) * 1024],
                        in_=z_sb[:, :, :]), "s_d", waits=[z_tok])
                if limit == "z":
                    p.barrier()
                    psC.close()
                    return

                tcs2 = {}
                for m in (0, 1):
                    tcs2[m] = p.gdma(lambda e, m=m: e.dma_start(
                        out=wgu_b[:, m, :, :], in_=dr["wo"].ap()[m]),
                        f"s_wg{m}",
                        waits=[slabs["gb"][m]] if slabs["gb"][m] else [])
                p.barrier(exclude=("gpsimd",))
                psC.close()
                g_rl = sb(f"grl{hh}", [128, 4, 512], BF16, ph)
                x_rl = sb(f"xrl{hh}", [128, 2, 512], F32, ph)
                o_sb = sb(f"osb{hh}", [128, 4, 512], F32, ph)
                grl_free = [None] * 4
                xrl_free = [None] * 2
                osb_free = [None] * 4
                drive2 = {}
                if hh == 0:
                    xt2f = sb("xt2f", [128, 16, 512], F32, ph)
                    xb2f = sb("xb2f", [128, 1, 16, 512], BF16, ph)
                    stats2f = sb("stats2f", [1, 3, 512], F32, ph)
                    chs2f = sb("chs2f", [128, 2, 512], F32, ph)
                    gwf2f = sb("gwf2f", [128, 16, 16], F32, ph)
                    rbx2f = sb("rbx2f", [16, 512], F32, ph)
                    tgw2 = p.dma(lambda e: e.dma_start(
                        out=gwf2f[:, :, :], in_=dr["gwt"].ap()), "s_x")
                    t2b = make_2a(1, xt2f, xb2f, stats2f, chs2f, (0, 0),
                                  (gwf2f, rbx2f, [tgw2]))
                    xtf = {"free": []}
                    t2b[0].stage0_sync([], [])

                    def a2f(tl):
                        t = ln2a[1]["tiles"][tl]
                        if tl == 1:
                            t.xb_wait0 = list(t2b[0].xb_free)
                        t.stage_a([])

                    def b2f(tl):
                        drive_2a_b(1, tl)
                        xtf["free"] = ln2a[1]["tiles"][tl].xt_free
                    drive2 = {2: (lambda: a2f(0)), 5: (lambda: b2f(0)),
                              7: (lambda: t2b[1].stage0_sync(xtf["free"], [])),
                              9: (lambda: a2f(1)),
                              12: (lambda: b2f(1))}
                for m in range(16):
                    if m in drive2:
                        drive2[m]()
                    sl = m % 2
                    tc = tcs2.pop(m, None)
                    if tc is None:
                        tc = p.gdma(lambda e, m=m, sl=sl: e.dma_start(
                            out=wgu_b[:, sl, :, :], in_=dr["wo"].ap()[m]),
                            f"s_wg{sl}",
                            waits=[slabs["gb"][sl]] if slabs["gb"][sl] else [])
                    tg = None
                    tls = []
                    for tt in range(2):
                        ot = (m * 2 + tt) % 4
                        tgl = p.dma(lambda e, m=m, hh=hh, tt=tt, ot=ot: e.dma_start(
                            out=g_rl[:, ot, :],
                            in_=bass.AP(dr["g_s"],
                                        m * 128 * 2048 + hh * 1024 + tt * 512,
                                        [[2048, 128], [1, 512]])), f"s_gr{ot}",
                            waits=[g_toks[m]] + ([grl_free[ot]] if grl_free[ot] else []))
                        xr = ot % 2
                        txl = p.dma(lambda e, m=m, hh=hh, tt=tt, xr=xr: e.dma_start(
                            out=x_rl[:, xr, :],
                            in_=bass.AP(dr["x_own"],
                                        m * 128 * 2048 + hh * 1024 + tt * 512,
                                        [[2048, 128], [1, 512]])), f"s_xr{xr}",
                            waits=[xrl_free[xr]] if xrl_free[xr] else [])
                        tls.append((tgl, txl, ot))
                    grp = [banks.get(), banks.get()]
                    for kc in range(16):
                        for tt in range(2):
                            tg = p.op("tensor", lambda e, bk=grp[tt][1], sl=sl, kc=kc, tt=tt: e.matmul(
                                bk[:, :], wgu_b[:, sl, kc, :],
                                z_sb[:, kc, tt * 512:(tt + 1) * 512],
                                start=(kc == 0), stop=(kc == 15)),
                                waits=[tc, z_tok] if kc == 0 else [],
                                inc="pe" if (kc == 15 and tt == 1) else None)
                    for tt in range(2):
                        bi, bk = grp[tt]
                        tgl, txl, ot = tls[tt]
                        tv1 = p.op("vector", lambda e, bk=bk, ot=ot: e.tensor_tensor(
                            out=o_sb[:, ot, :], in0=bk[:, :], in1=g_rl[:, ot, :],
                            op=ALU.mult),
                            waits=[tg, tgl] + ([osb_free[ot]] if osb_free[ot] else []),
                            inc="v")
                        grl_free[ot] = tv1
                        tv2 = p.op("vector", lambda e, ot=ot, xr=ot % 2: e.tensor_tensor(
                            out=o_sb[:, ot, :], in0=o_sb[:, ot, :], in1=x_rl[:, xr, :],
                            op=ALU.add), waits=[txl, tv1], inc="v")
                        xrl_free[ot % 2] = tv2
                        banks.set_release(bi, tv1)
                        tst = p.dma(lambda e, m=m, hh=hh, tt=tt, ot=ot: e.dma_start(
                            out=bass.AP(dr["out_fm"],
                                        m * 128 * 2048 + hh * 1024 + tt * 512,
                                        [[2048, 128], [1, 512]]),
                            in_=o_sb[:, ot, :]), f"s_o{ot}", waits=[tv2])
                        osb_free[ot] = tst
                    slabs["gb"][sl] = tg
                p.barrier()

        for hh in range(2):
            emit_phase2(hh)

    for i in range(4):
        p.wait("sync", p.now(f"s_o{i}"))
    p.wait("sync", p.now("s_d"))
    p.replay()
    nc._keepalive = st
    return nc


_CACHE = {}


def kernel(**inputs):
    if "nc" not in _CACHE:
        _CACHE["nc"] = _build()
    nc = _CACHE["nc"]
    in_maps = _host_prep(inputs)
    res = run_bass_kernel_spmd(nc, in_maps, core_ids=list(range(NCORE)))
    return _assemble(res.results)


# version 20
# speedup vs baseline: 1.0837x; 1.0837x over previous
import numpy as np
import ml_dtypes
from contextlib import ExitStack

import concourse.bass as bass
import concourse.mybir as mybir
from concourse.bass_utils import run_bass_kernel_spmd

F32 = mybir.dt.float32
BF16 = mybir.dt.bfloat16
AF = mybir.ActivationFunctionType
ALU = mybir.AluOpType
AX = mybir.AxisListType
bfnp = ml_dtypes.bfloat16

B, S, D = 4, 4096, 2048
E = 16
NCORE = 8
NEG = -1.0e30
QSCALE = 1.0 / float(np.sqrt(128.0))
SLOPES = [float(2.0 ** (-(i + 1) / 2.0)) for i in range(16)]


class Prog:
    ENGINES = ("sync", "tensor", "vector", "scalar", "gpsimd")

    def __init__(self, nc, stack):
        self.nc = nc
        self.stack = stack
        self.items = {e: [] for e in self.ENGINES}
        self._sems = {}
        self.counts = {}
        self.last_wait = {e: {} for e in self.ENGINES}

    def sem(self, name):
        if name not in self._sems:
            self._sems[name] = self.stack.enter_context(self.nc.semaphore(name))
            self.counts[name] = 0
        return self._sems[name]

    def wait(self, eng, tok):
        if tok is None:
            return
        name, val = tok
        if val <= 0:
            return
        if val <= self.last_wait[eng].get(name, 0):
            return
        self.last_wait[eng][name] = val
        self.items[eng].append(("w", self.sem(name), val))

    def op(self, eng, fn, waits=(), inc=None, incby=1):
        for tok in waits:
            self.wait(eng, tok)
        if inc is not None:
            s = self.sem(inc)
            self.counts[inc] += incby
            self.items[eng].append(("o", fn, s, incby))
            return (inc, self.counts[inc])
        self.items[eng].append(("o", fn, None, 0))
        return None

    def dma(self, fn, stream, waits=()):
        return self.op("sync", fn, waits=waits, inc=stream, incby=16)

    def gdma(self, fn, stream, waits=()):
        return self.op("gpsimd", fn, waits=waits, inc=stream, incby=16)

    def now(self, name):
        return (name, self.counts.get(name, 0))

    def barrier(self, exclude=()):
        toks = [self.now(n) for n in self._sems]
        for eng in self.ENGINES:
            if eng in exclude:
                continue
            for tok in toks:
                self.wait(eng, tok)

    def replay(self):
        with self.nc.Block() as block:
            for eng in self.ENGINES:
                if not self.items[eng]:
                    continue
                items = self.items[eng]

                def run(e, items=items):
                    for it in items:
                        if it[0] == "w":
                            e.wait_ge(it[1], it[2])
                        else:
                            ins = it[1](e)
                            if it[2] is not None:
                                ins.then_inc(it[2], it[3])

                getattr(block, eng)(run)


class Banks:

    def __init__(self, prog, banks):
        self.prog = prog
        self.banks = banks
        self.release = [None] * len(banks)
        self.held = [False] * len(banks)
        self.rr = 0

    def get(self, eng="tensor", hold=False):
        for _ in range(len(self.banks)):
            i = self.rr % len(self.banks)
            self.rr += 1
            if not self.held[i]:
                break
        else:
            raise RuntimeError("all banks held")
        if self.release[i] is not None:
            self.prog.wait(eng, self.release[i])
        self.held[i] = hold
        return i, self.banks[i]

    def set_release(self, i, tok):
        self.release[i] = tok
        self.held[i] = False


def _pretile_lhsT(w_in_out, n_m, n_kc):
    a = w_in_out.reshape(n_kc, 128, n_m, 128)
    return np.ascontiguousarray(a.transpose(2, 1, 0, 3))


def _host_prep(inputs):
    x = np.ascontiguousarray(np.asarray(inputs["L0_attn_resid_pre"], dtype=np.float32))
    Wq = np.asarray(inputs["L0_attn_Wq"], dtype=np.float32)
    Wk = np.asarray(inputs["L0_attn_Wk"], dtype=np.float32)
    Wv = np.asarray(inputs["L0_attn_Wv"], dtype=np.float32)
    Wo = np.asarray(inputs["L0_attn_Wo"], dtype=np.float32)
    gw = np.asarray(inputs["L0_ffn_gate_w"], dtype=np.float32)
    ebias = np.asarray(inputs["L0_ffn_expert_bias"], dtype=np.float32)
    sgw = np.asarray(inputs["L0_ffn_shared_gate_w"], dtype=np.float32)
    suw = np.asarray(inputs["L0_ffn_shared_up_w"], dtype=np.float32)
    sdw = np.asarray(inputs["L0_ffn_shared_down_w"], dtype=np.float32)
    egu = np.asarray(inputs["L0_ffn_experts_gate_up"], dtype=np.float32)
    edn = np.asarray(inputs["L0_ffn_experts_down"], dtype=np.float32)

    sh = {}
    sh["wq"] = _pretile_lhsT(np.ascontiguousarray(Wq.T), 16, 16)
    sh["wk"] = _pretile_lhsT(np.ascontiguousarray(Wk.T), 16, 16)
    sh["wv"] = np.ascontiguousarray(Wv.T.reshape(16, 128, 2048))
    sh["wo"] = _pretile_lhsT(np.ascontiguousarray(Wo.T), 16, 16)
    sh["sgw"] = _pretile_lhsT(sgw, 8, 16)
    sh["suw"] = _pretile_lhsT(suw, 8, 16)
    sh["sdw"] = _pretile_lhsT(sdw, 16, 8)
    sh["egu"] = np.ascontiguousarray(
        egu.reshape(E, 16, 128, 16, 128).transpose(0, 3, 2, 1, 4))
    sh["edn"] = np.ascontiguousarray(
        edn.reshape(E, 8, 128, 16, 128).transpose(0, 3, 2, 1, 4))
    sh["gwt"] = np.ascontiguousarray(gw.reshape(16, 128, 16).transpose(1, 0, 2))
    sh["ebias"] = np.ascontiguousarray(ebias.reshape(16, 1))

    oh = np.zeros((16, 2048), dtype=np.float32)
    for e in range(16):
        oh[e, e * 128:(e + 1) * 128] = 1.0
    sh["onehot"] = oh
    jj = np.arange(128)
    al = np.zeros((128, 512), dtype=np.float32)
    for h in range(16):
        for kc in range(32):
            al[:, h * 32 + kc] = np.float32(SLOPES[h]) * (128.0 * kc + jj)
    sh["alibi"] = al
    sh["ident"] = np.eye(128, dtype=np.float32)
    sh["ones_row"] = np.ones((1, 128), dtype=np.float32)
    ze = np.zeros((128, 2), dtype=np.float32); ze[:, 1] = 1e-5
    sh["zeps"] = ze
    sm = np.zeros((128, 2048), dtype=np.float32)
    sm[0, :] = np.repeat(np.array(SLOPES, dtype=np.float32), 128)
    sh["slopes_mat"] = sm.astype(bfnp)
    sh["ones_mat"] = np.ones((128, 128), dtype=bfnp)
    sh["ones_col"] = np.ones((128, 1), dtype=bfnp)

    in_maps = []
    ii = np.arange(128)
    for c in range(NCORE):
        b, par = c // 2, c % 2
        m = dict(sh)
        xb = x[b]
        m["x_fm"] = np.ascontiguousarray(xb.T).reshape(16, 128, 4096)
        xo = np.ascontiguousarray(xb.reshape(16, 2, 128, 2048)[:, par])
        m["x_own"] = np.ascontiguousarray(
            xo.transpose(2, 0, 1).reshape(16, 128, 2048))
        msk = np.zeros((128, 8, 4, 128), dtype=np.float32)
        for r in range(8):
            for v in range(4):
                Dv = 2 * v + par - r
                if Dv >= 1:
                    continue
                elif Dv == 0:
                    msk[:, r, v, :] = np.where(jj[:, None] <= ii[None, :], 0.0, NEG)
                else:
                    msk[:, r, v, :] = NEG
        m["masks"] = msk.reshape(128, 8, 512).astype(bfnp)
        nq = np.zeros((1, 2048), dtype=np.float32)
        for T in range(4):
            for v in range(4):
                u = 4 * T + v
                nq[0, T * 512 + v * 128:T * 512 + (v + 1) * 128] = -(
                    256.0 * u + 128.0 * par + ii)
        nq2 = np.zeros((128, 2048), dtype=np.float32)
        nq2[0] = nq[0]
        m["negq_bf"] = nq2.astype(bfnp)
        in_maps.append(m)
    return in_maps


def _assemble(results):
    out = np.empty((B, S, D), dtype=np.float32)
    for c in range(NCORE):
        b, par = c // 2, c % 2
        o = results[c]["out_fm"]
        arr = o.reshape(16, 128, 16, 128).transpose(2, 3, 0, 1)
        out[b].reshape(16, 2, 128, 2048)[:, par] = arr.reshape(16, 128, 2048)
    return out


def _build(dump=(), phases=("p1", "p2"), limit=None):
    nc = bass.Bass("TRN2", target_bir_lowering=False)
    st = ExitStack()
    p = Prog(nc, st)

    dr = {}
    for nm, shp, dt in [
        ("x_fm", [16, 128, 4096], F32), ("x_own", [16, 128, 2048], F32),
        ("wq", [16, 128, 16, 128], F32), ("wk", [16, 128, 16, 128], F32),
        ("wv", [16, 128, 2048], F32), ("wo", [16, 128, 16, 128], F32),
        ("sgw", [8, 128, 16, 128], F32), ("suw", [8, 128, 16, 128], F32),
        ("sdw", [16, 128, 8, 128], F32),
        ("egu", [E, 16, 128, 16, 128], F32), ("edn", [E, 16, 128, 8, 128], F32),
        ("gwt", [128, 16, 16], F32), ("ebias", [16, 1], F32),
        ("onehot", [16, 2048], F32),
        ("alibi", [128, 512], F32), ("ident", [128, 128], F32),
        ("ones_row", [1, 128], F32), ("ones_col", [128, 1], BF16),
        ("zeps", [128, 2], F32), ("slopes_mat", [128, 2048], BF16),
        ("ones_mat", [128, 128], BF16), ("negq_bf", [128, 2048], BF16),
        ("masks", [128, 8, 512], BF16),
    ]:
        dr[nm] = nc.dram_tensor(nm, shp, dt, kind="ExternalInput")
    dr["out_fm"] = nc.dram_tensor("out_fm", [16, 128, 2048], F32, kind="ExternalOutput")
    dr["k_s"] = nc.dram_tensor("k_s", [16, 8, 128, 512], BF16)
    dr["v_s"] = nc.dram_tensor("v_s", [16, 32, 128, 128], BF16)
    dr["g_s"] = nc.dram_tensor("g_s", [16, 128, 2048], BF16)
    dumps = {}
    for nm in dump:
        shp = {"k_s": [16, 8, 128, 512], "v_s": [16, 32, 128, 128],
               "g_s": [16, 128, 2048], "logits": [16, 2048],
               "xn": [16, 128, 2048], "q": [128, 16, 2048], "z": [128, 16, 2048],
               "comb": [16, 2048], "acc": [128, 16, 2048],
               "sgt": [128, 8, 16], "sbt": [128, 8, 16],
               "tops": [128, 8, 8]}[nm]
        dt = {"logits": F32, "acc": F32, "comb": F32, "sgt": F32, "sbt": F32, "tops": F32}.get(nm, BF16)
        dumps[nm] = nc.dram_tensor("dump_" + nm, shp, dt, kind="ExternalOutput")

    SB = {}

    def sb(name, shape, dt, stack=st):
        SB[name] = stack.enter_context(nc.sbuf_tensor(name, shape, dt))
        return SB[name]

    ebias_sb = sb("ebias_sb", [16, 1], F32)
    ident_sb = sb("ident_sb", [128, 128], F32)
    ones_row = sb("onesr_sb", [1, 128], F32)
    ones_col = sb("onesc_sb", [128, 1], BF16)
    zeps_sb = sb("zeps_sb", [128, 2], F32)
    onesm_sb = sb("onesm_sb", [128, 128], BF16)
    xnh = sb("xnh", [128, 2, 16, 1024], BF16)
    logit = sb("logitg", [16, 2, 1024], F32)
    qprev_pe = [None, None]

    PS = [st.enter_context(nc.psum_tensor(f"bank{i}", [128, 512], F32)) for i in range(8)]
    banks = Banks(p, PS)

    for nm, dst in [("ident", ident_sb), ("ones_row", ones_row),
                    ("ones_col", ones_col), ("ebias", ebias_sb),
                    ("zeps", zeps_sb), ("ones_mat", onesm_sb)]:
        p.dma(lambda e, a=dr[nm], d=dst: e.dma_start(
            out=d[tuple(slice(None) for _ in d.shape)], in_=a.ap()), "s_c")
    c_tok = p.now("s_c")

    p.barrier()

    class LNTile:

        def __init__(self, xt, slot, xb, stats, chs, loch, xn_dst_slices,
                     x_src_ap, router=False, xn_wait=(), loch_wait=()):
            self.xt, self.slot, self.xb = xt, slot, xb
            self.stats, self.chs, self.loch = stats, chs, loch
            self.xn_dst = xn_dst_slices
            self.x_src = x_src_ap
            self.router = router
            self.xn_wait = list(xn_wait)
            self.loch_wait = list(loch_wait)
            self.rt_tok = None

        def stage0(self, xb_wait):
            self.txb = p.gdma(lambda e: e.dma_start(
                out=self.xb[:, self.slot, :, :], in_=self.x_src),
                f"s_xb{self.slot}", waits=xb_wait)

        def stage0_sync(self, xt_wait, xb_wait):
            self.sync_mode = True
            self.tx = p.dma(lambda e: e.dma_start(
                out=self.xt[:, :, :], in_=self.x_src), "s_x", waits=xt_wait)
            self.xb_wait0 = list(xb_wait)

        def stage_a(self, xt_wait):
            xb, slot = self.xb, self.slot
            self.pe_prev = p.now("pe")
            self.v_prev = p.now("v")
            if getattr(self, "sync_mode", False):
                self.txb = p.op("scalar", lambda e: e.copy(
                    out=xb[:, slot, :, :], in_=self.xt[:, :, :]),
                    waits=[self.tx] + self.xb_wait0, inc="a")
            else:
                self.tx = p.dma(lambda e: e.dma_start(
                    out=self.xt[:, :, :], in_=self.x_src), "s_x", waits=xt_wait)
            self.bi_s, self.bk_s = banks.get(hold=True)
            ts_sum = None
            for kc in range(16):
                ts_sum = p.op("tensor", lambda e, bk=self.bk_s, kc=kc: e.matmul(
                    bk[0:1, :], ones_col[:, :], xb[:, slot, kc, :],
                    start=(kc == 0), stop=(kc == 15)),
                    waits=[self.txb, c_tok] if kc == 0 else [],
                    inc="pe" if kc == 15 else None)
            self.ts_sum = ts_sum
            self.tsq = p.op("scalar", lambda e: e.activation(
                out=xb[:, slot, :, :], in_=xb[:, slot, :, :], func=AF.Square,
                bias=zeps_sb[:, 0:1], scale=1.0),
                waits=[self.txb, ts_sum, self.pe_prev], inc="a")

        def stage_b(self):
            xb, slot, stats, chs = self.xb, self.slot, self.stats, self.chs
            xt, loch, router = self.xt, self.loch, self.router
            xn_dst_slices, tx = self.xn_dst, self.tx
            bi_q, bk_q = banks.get()
            ts_sq = None
            for kc in range(16):
                ts_sq = p.op("tensor", lambda e, bk=bk_q, kc=kc: e.matmul(
                    bk[0:1, :], ones_col[:, :], xb[:, slot, kc, :],
                    start=(kc == 0), stop=(kc == 15)),
                    waits=[self.tsq] if kc == 0 else [],
                    inc="pe" if kc == 15 else None)
            tm = p.op("scalar", lambda e, bk=self.bk_s: e.activation(
                out=stats[:, 0, :], in_=bk[0:1, :], func=AF.Copy,
                scale=1.0 / 2048.0, bias=0.0),
                waits=[self.ts_sum, self.v_prev], inc="a")
            te2 = p.op("scalar", lambda e, bk=bk_q: e.activation(
                out=stats[:, 1, :], in_=bk[0:1, :], func=AF.Copy,
                scale=1.0 / 2048.0, bias=0.0), waits=[ts_sq], inc="a")
            banks.set_release(self.bi_s, tm)
            banks.set_release(bi_q, te2)
            tv1 = p.op("vector", lambda e: e.tensor_tensor(
                out=stats[:, 2, :], in0=stats[:, 0, :], in1=stats[:, 0, :],
                op=ALU.mult), waits=[tm], inc="v")
            tv2 = p.op("vector", lambda e: e.tensor_tensor(
                out=stats[:, 2, :], in0=stats[:, 1, :], in1=stats[:, 2, :],
                op=ALU.subtract), waits=[te2, tv1], inc="v")
            tsq2 = p.op("scalar", lambda e: e.activation(
                out=stats[:, 1, :], in_=stats[:, 2, :], func=AF.Sqrt,
                bias=zeps_sb[0:1, 1:2], scale=1.0), waits=[tv2], inc="a")
            trr = p.op("vector", lambda e: e.reciprocal(
                out=stats[:, 1, :], in_=stats[:, 1, :]), waits=[tsq2], inc="v")
            bi_m, bk_m = banks.get(hold=True)
            tbm = p.op("tensor", lambda e, bk=bk_m: e.matmul(
                bk[:, :], ones_row[:, :], stats[:, 0, :], start=True, stop=True),
                waits=[tm], inc="pe")
            bi_r, bk_r = banks.get(hold=True)
            tbr = p.op("tensor", lambda e, bk=bk_r: e.matmul(
                bk[:, :], ones_row[:, :], stats[:, 1, :], start=True, stop=True),
                waits=[trr], inc="pe")
            if router:
                gw_f, rbx, logit_dst, gw_ld = self.router_ctx
                bi_rt, bk_rt = banks.get(hold=True)
                bi_rb, bk_rb = banks.get()
                trb = p.op("tensor", lambda e, bk=bk_rb: e.matmul(
                    bk[0:16, :], ones_row[0:1, 0:16], stats[:, 1, :],
                    start=True, stop=True), waits=[trr], inc="pe")
                trbe = p.op("scalar", lambda e, bk=bk_rb: e.copy(
                    out=rbx[:, :], in_=bk[0:16, :]), waits=[trb], inc="a")
                banks.set_release(bi_rb, trbe)
            xn_toks = []
            last_reader = None
            mm_toks = {}
            for kc in range(16):
                sl2 = kc % 2
                wk1 = ([mm_toks[kc - 2]] if kc >= 2 else
                       list(self.loch_wait)) if router else []
                td1 = p.op("vector", lambda e, kc=kc, sl2=sl2, bk=bk_m: e.tensor_tensor(
                    out=chs[:, sl2, :], in0=xt[:, kc, :], in1=bk[:, :],
                    op=ALU.subtract),
                    waits=[tbm, tx] + wk1 +
                    ([last_reader] if last_reader else []),
                    inc="v")
                td2 = p.op("vector", lambda e, kc=kc, sl2=sl2, bk=bk_r: e.tensor_tensor(
                    out=xn_dst_slices(kc), in0=chs[:, sl2, :], in1=bk[:, :],
                    op=ALU.mult), waits=[tbr, td1] + self.xn_wait, inc="v")
                xn_toks.append(td2)
                last_reader = td2
                if router:
                    mm_toks[kc] = p.op("tensor", lambda e, kc=kc, sl2=sl2, bk=bk_rt: e.matmul(
                        bk[0:16, :], gw_f[:, kc, :], chs[:, sl2, :],
                        start=(kc == 0), stop=(kc == 15)),
                        waits=[td1] + (gw_ld if kc == 0 else []), inc="pe")
            if router:
                self.rt_tok = p.op("vector", lambda e, bk=bk_rt: e.tensor_tensor(
                    out=logit_dst, in0=bk[0:16, :], in1=rbx[:, :],
                    op=ALU.mult), waits=[mm_toks[15], trbe], inc="v")
                banks.set_release(bi_rt, self.rt_tok)
                self.mm_last = [mm_toks[14], mm_toks[15]]
            else:
                self.rt_tok = None
            banks.set_release(bi_m, xn_toks[-1])
            banks.set_release(bi_r, p.now("v"))
            self.xn_toks = xn_toks
            self.xb_free = [self.tsq, ts_sq]
            self.xt_free = [p.now("v")]

    def emit_ln_tile(xt, slot, xb, stats, chs, loch, xn_dst_slices, x_src_ap,
                     xt_wait, xb_wait, router=None, xn_wait=()):
        t = LNTile(xt, slot, xb, stats, chs, loch, xn_dst_slices, x_src_ap,
                   router=router, xn_wait=xn_wait)
        t.stage0(xb_wait)
        t.stage_a(xt_wait)
        t.stage_b()
        return t.xt_free, t.xb_free, t.xn_toks, t.rt_tok

    ln2a = {}

    def make_2a(hh, xt_, xb_, stats_, chs_, slots, router_ctx):
        tiles = []
        for tl in range(2):
            src = dr["x_own"].ap()[:, :, hh * 1024 + tl * 512:
                                   hh * 1024 + (tl + 1) * 512] \
                .rearrange("c f s -> f c s")
            gw_f, rbx, gw_ld = router_ctx
            tiles.append(LNTile(
                xt_, slots[tl], xb_, stats_, chs_, None,
                lambda kc, hh=hh, tl=tl: xnh[:, hh, kc,
                                             tl * 512:(tl + 1) * 512],
                src, router=True,
                xn_wait=[qprev_pe[hh]] if qprev_pe[hh] else []))
            tiles[tl].router_ctx = (
                gw_f, rbx,
                logit[:, hh, tl * 512:(tl + 1) * 512], gw_ld)
        ln2a[hh] = {"tiles": tiles, "lg": [], "xn": []}
        return tiles

    def drive_2a_b(hh, tl):
        t = ln2a[hh]["tiles"][tl]
        if tl == 1:
            t.loch_wait = ln2a[hh]["tiles"][0].mm_last
        t.stage_b()
        ln2a[hh]["xn"].append(t.xn_toks[-1])
        ln2a[hh]["lg"].append(t.rt_tok)

    if "p1" in phases:
        with ExitStack() as ph:
            xt = sb("xt", [128, 16, 512], F32, ph)
            xb = sb("xb", [128, 2, 16, 512], BF16, ph)
            stats = sb("stats", [1, 3, 512], F32, ph)
            chs = sb("chs", [128, 2, 512], F32, ph)
            gwf = sb("gwf", [128, 16, 16], F32, ph)
            rbx1 = sb("rbx1", [16, 512], F32, ph)
            tgwl = p.dma(lambda e: e.dma_start(
                out=gwf[:, :, :], in_=dr["gwt"].ap()), "s_gw1")
            gw_tok = tgwl
            wslab_b = sb("wslab_b", [128, 2, 16, 128], BF16, ph)
            wvb = sb("wvb", [128, 4, 512], BF16, ph)
            kev = sb("kev", [128, 2, 512], BF16, ph)
            vev = sb("vev", [128, 8, 512], BF16, ph)

            xb_free = [[], []]
            kb_free = [None, None]
            vb_free = [None, None, None, None]
            kev_free = [None, None]
            vev_free = [None] * 8
            xt_state = {"free": []}

            def prep_quarter(qt):
                qs = qt % 2
                objs = []
                for tl in range(2):
                    stg = qt * 2 + tl
                    xsrc = dr["x_fm"].ap()[:, :, stg * 512:(stg + 1) * 512] \
                        .rearrange("c f s -> f c s")
                    objs.append(LNTile(
                        xt, tl, xb, stats, chs, None,
                        lambda kc, tl=tl, qs=qs: xnh[:, qs, kc,
                                                     tl * 512:(tl + 1) * 512],
                        xsrc, router=None,
                        xn_wait=[qprev_pe[qs]] if qprev_pe[qs] else []))
                return objs

            def _ln_a(t):
                t.stage_a(xt_state["free"])

            def _ln_b(t):
                t.stage_b()
                xt_state["free"] = t.xt_free

            wk_pre = {}

            def issue_wk(qt, m):
                wk_pre[(qt, m)] = p.gdma(lambda e, m=m: e.dma_start(
                    out=wslab_b[:, m % 2, :, :], in_=dr["wk"].ap()[m]),
                    f"s_wk{m % 2}",
                    waits=[kb_free[m % 2]] if kb_free[m % 2] else [])

            def kv_quarter(qt, xn_tok, drive, pre=None, post=None, mid=None):
                qs = qt % 2
                for m in (0, 1):
                    if (qt, m) not in wk_pre:
                        issue_wk(qt, m)
                if pre is not None:
                    pre()
                for m in range(16):
                    if m in drive:
                        drive[m]()
                    sl = m % 2
                    tc = wk_pre.pop((qt, m), None)
                    if tc is None:
                        tc = p.gdma(lambda e, m=m, sl=sl: e.dma_start(
                            out=wslab_b[:, sl, :, :], in_=dr["wk"].ap()[m]),
                            f"s_wk{sl}",
                            waits=[kb_free[sl]] if kb_free[sl] else [])
                    grp = [banks.get(), banks.get()]
                    tg = None
                    ktok = [None, None]
                    for kc in range(16):
                        for tt in range(2):
                            tg = p.op("tensor", lambda e, bk=grp[tt][1], sl=sl, kc=kc, tt=tt, qs=qs: e.matmul(
                                bk[:, :], wslab_b[:, sl, kc, :],
                                xnh[:, qs, kc, tt * 512:(tt + 1) * 512],
                                start=(kc == 0), stop=(kc == 15)),
                                waits=[tc, xn_tok[tt]] if kc == 0 else [],
                                inc="pe" if kc == 15 else None)
                            if kc == 15:
                                ktok[tt] = tg
                    kb_free[sl] = tg
                    for tt in range(2):
                        bi, bk = grp[tt]
                        ev = kev[:, tt, :]
                        tev = p.op("scalar", lambda e, bk=bk, ev=ev: e.copy(
                            out=ev, in_=bk[:, :]),
                            waits=[ktok[tt]] + ([kev_free[tt]] if kev_free[tt] else []),
                            inc="a")
                        banks.set_release(bi, tev)
                        seg = qt * 2 + tt
                        tdm = p.dma(lambda e, m=m, seg=seg, ev=ev: e.dma_start(
                            out=bass.AP(dr["k_s"], m * 524288 + seg * 65536,
                                        [[512, 128], [1, 512]]),
                            in_=ev), f"s_spk{tt}", waits=[tev])
                        kev_free[tt] = tdm
                for ct in range(4):
                    vgrp = []
                    for tb in range(8):
                        vgrp.append(banks.get())
                    tg = None
                    vtok = [None] * 8
                    for kc in range(16):
                        sl = kc % 4
                        tc = p.gdma(lambda e, kc=kc, ct=ct, sl=sl: e.dma_start(
                            out=wvb[:, sl, :],
                            in_=bass.AP(dr["wv"], kc * 128 * 2048 + ct * 512,
                                        [[2048, 128], [1, 512]])), f"s_wv{sl}",
                            waits=[vb_free[sl]] if vb_free[sl] else [])
                        for tb in range(8):
                            tg = p.op("tensor", lambda e, bk=vgrp[tb][1], kc=kc, tb=tb, sl=sl, qs=qs: e.matmul(
                                bk[:, :], xnh[:, qs, kc, tb * 128:(tb + 1) * 128],
                                wvb[:, sl, :], start=(kc == 0), stop=(kc == 15)),
                                waits=([tc, xn_tok[0]] if tb == 0 else
                                       ([xn_tok[1]] if (tb == 4 and kc == 0)
                                        else [])),
                                inc="pe" if (tb == 7 or kc == 15) else None)
                            if kc == 15:
                                vtok[tb] = tg
                        vb_free[sl] = tg
                    for tb in range(8):
                        bi, bk = vgrp[tb]
                        ev = vev[:, tb, :]
                        tev = p.op("scalar", lambda e, bk=bk, ev=ev: e.copy(
                            out=ev, in_=bk[:, :]),
                            waits=[vtok[tb]] + ([vev_free[tb]] if vev_free[tb] else []),
                            inc="a")
                        banks.set_release(bi, tev)
                        kb = qt * 8 + tb
                        tdm = p.dma(lambda e, kb=kb, ct=ct, ev=ev: e.dma_start(
                            out=bass.AP(dr["v_s"], 4 * ct * 524288 + kb * 16384,
                                        [[128, 128], [524288, 4], [1, 128]]),
                            in_=ev.rearrange("kk (hh dh) -> kk hh dh", hh=4)),
                            f"s_spv{tb}", waits=[tev])
                        vev_free[tb] = tdm
                    if ct == 0 and mid is not None:
                        mid()
                if post is not None:
                    post()
                qprev_pe[qs] = p.now("pe")

            cur = prep_quarter(0)
            for tl in range(2):
                cur[tl].stage0(xb_free[tl])
                _ln_a(cur[tl])
                _ln_b(cur[tl])
            xn_tok = (cur[0].xn_toks[-1], cur[1].xn_toks[-1])
            for qt in range(4):
                drive, pre, nxt, post, mid = {}, None, None, None, None
                if qt + 1 < 4:
                    nxt = prep_quarter(qt + 1)

                    def pre(nxt=nxt, cur=cur):
                        nxt[0].stage0(cur[0].xb_free)
                        nxt[1].stage0(cur[1].xb_free)
                    drive = {4: (lambda t=nxt[0]: _ln_a(t)),
                             8: (lambda t=nxt[0]: _ln_b(t)),
                             11: (lambda t=nxt[1]: _ln_a(t)),
                             14: (lambda t=nxt[1]: _ln_b(t))}
                    mid = (lambda qt=qt: (issue_wk(qt + 1, 0),
                                          issue_wk(qt + 1, 1)))
                else:
                    t2a = make_2a(0, xt, xb, stats, chs, (0, 1),
                                  (gwf, rbx1, [gw_tok]))

                    def pre(t2a=t2a, cur=cur):
                        t2a[0].stage0(cur[0].xb_free)
                        t2a[1].stage0(cur[1].xb_free)

                    def a2(tl):
                        ln2a[0]["tiles"][tl].stage_a(xt_state["free"])

                    def b2(tl):
                        drive_2a_b(0, tl)
                        xt_state["free"] = ln2a[0]["tiles"][tl].xt_free
                    drive = {4: (lambda: a2(0)), 8: (lambda: b2(0)),
                             11: (lambda: a2(1)), 14: (lambda: b2(1))}
                kv_quarter(qt, xn_tok, drive, pre, post, mid)
                if nxt is not None:
                    cur = nxt
                    xn_tok = (nxt[0].xn_toks[-1], nxt[1].xn_toks[-1])
            p.barrier()
        if "k_s" in dumps:
            p.dma(lambda e: e.dma_start(out=dumps["k_s"].ap(), in_=dr["k_s"].ap()), "s_d")
        if "v_s" in dumps:
            p.dma(lambda e: e.dma_start(out=dumps["v_s"].ap(), in_=dr["v_s"].ap()), "s_d")

    if "p2" in phases:
        def emit_phase2(hh):
            with ExitStack() as ph:
                sb(f"shim_ph{hh}", [128, 8 + 64 * hh], F32, ph)
                wgu_b = sb(f"wgub{hh}", [128, 2, 16, 128], BF16, ph)

                xn_tok = tuple(ln2a[hh]["xn"])
                lg_tok = ln2a[hh]["lg"]
                if "xn" in dumps:
                    for kc in range(16):
                        p.dma(lambda e, hh=hh, kc=kc: e.dma_start(
                            out=dumps["xn"].ap()[kc, :, hh * 1024:(hh + 1) * 1024],
                            in_=xnh[:, hh, kc, :]), "s_d", waits=list(xn_tok))
                if "logits" in dumps:
                    p.dma(lambda e, hh=hh: e.dma_start(
                        out=dumps["logits"].ap()[:, hh * 1024:(hh + 1) * 1024],
                        in_=logit[:, hh, :]), "s_d", waits=lg_tok)

                psB = ExitStack()
                sb(f"shim_psB{hh}", [128, 32], F32, psB)
                acc = sb(f"acc{hh}", [128, 16, 1024], F32, psB)
                gsil = sb(f"gsil{hh}", [128, 2, 1024], BF16, psB)
                comb_sb = sb(f"comb{hh}", [16, 1024], F32, psB)
                cb_sb = sb(f"cb{hh}", [128, 1024], F32, psB)
                onehot_sb = sb(f"onehot{hh}", [16, 2048], F32, psB)
                p.dma(lambda e: e.dma_start(
                    out=onehot_sb[:, :], in_=dr["onehot"].ap()), f"s_oh{hh}")
                oh_tok = p.now(f"s_oh{hh}")

                ps3 = ExitStack()
                sb(f"shim_ps3{hh}", [128, 24], F32, ps3)
                sg_sb = sb(f"sg{hh}", [16, 1024], F32, ps3)
                sbb_sb = sb(f"sbb{hh}", [16, 1024], F32, ps3)
                sgT = sb(f"sgT{hh}", [128, 8, 16], F32, ps3)
                sbT = sb(f"sbT{hh}", [128, 8, 16], F32, ps3)
                tops = sb(f"tops{hh}", [128, 8, 8], F32, ps3)
                tscr = sb(f"tscr{hh}", [128, 5, 16], F32, ps3)

                tsg = p.op("scalar", lambda e: e.activation(
                    out=sg_sb[:, :], in_=logit[:, hh, :], func=AF.Sigmoid,
                    bias=zeps_sb[0:16, 0:1],
                    scale=1.0), waits=lg_tok, inc="a")
                tsb = p.op("vector", lambda e: e.tensor_scalar(
                    out=sbb_sb[:, :], in0=sg_sb[:, :], scalar1=ebias_sb[:, :],
                    scalar2=None, op0=ALU.add), waits=[tsg, c_tok], inc="v")
                for blk in range(8):
                    bi1, bk1 = banks.get()
                    tt1 = p.op("tensor", lambda e, bk=bk1, blk=blk: e.transpose(
                        bk[0:128, 0:16], sg_sb[:, blk * 128:(blk + 1) * 128],
                        ident_sb[0:16, 0:16]), waits=[tsg], inc="pe")
                    te1 = p.op("scalar", lambda e, bk=bk1, blk=blk: e.copy(
                        out=sgT[:, blk, :], in_=bk[0:128, 0:16]), waits=[tt1], inc="a")
                    banks.set_release(bi1, te1)
                    bi2, bk2 = banks.get()
                    tt2 = p.op("tensor", lambda e, bk=bk2, blk=blk: e.transpose(
                        bk[0:128, 0:16], sbb_sb[:, blk * 128:(blk + 1) * 128],
                        ident_sb[0:16, 0:16]), waits=[tsb], inc="pe")
                    te2_ = p.op("scalar", lambda e, bk=bk2, blk=blk: e.copy(
                        out=sbT[:, blk, :], in_=bk[0:128, 0:16]), waits=[tt2], inc="a")
                    banks.set_release(bi2, te2_)
                    m1 = tops[:, blk, 0:1]; m2 = tops[:, blk, 1:2]
                    s1 = tops[:, blk, 2:3]; s2 = tops[:, blk, 3:4]
                    rec = tops[:, blk, 4:5]; w1 = tops[:, blk, 5:6]; w2 = tops[:, blk, 6:7]
                    eq1 = tscr[:, 0, :]; mskd = tscr[:, 1, :]; tmul = tscr[:, 2, :]
                    eq2 = tscr[:, 3, :]
                    x_ = sbT[:, blk, :]; g_ = sgT[:, blk, :]
                    ops = [
                        lambda e, x_=x_, m1=m1: e.reduce_max(m1, x_, axis=AX.X),
                        lambda e, x_=x_, m1=m1, eq1=eq1: e.tensor_scalar(
                            out=eq1, in0=x_, scalar1=m1, scalar2=None, op0=ALU.is_equal),
                        lambda e, x_=x_, eq1=eq1, mskd=mskd: e.scalar_tensor_tensor(
                            out=mskd, in0=eq1, scalar=NEG, in1=x_, op0=ALU.mult,
                            op1=ALU.add),
                        lambda e, m2=m2, mskd=mskd: e.reduce_max(m2, mskd, axis=AX.X),
                        lambda e, g_=g_, eq1=eq1, tmul=tmul: e.tensor_tensor(
                            out=tmul, in0=g_, in1=eq1, op=ALU.mult),
                        lambda e, s1=s1, tmul=tmul: e.reduce_sum(s1, tmul, axis=AX.X),
                        lambda e, m2=m2, mskd=mskd, eq2=eq2: e.tensor_scalar(
                            out=eq2, in0=mskd, scalar1=m2, scalar2=None,
                            op0=ALU.is_equal),
                        lambda e, g_=g_, eq2=eq2, tmul=tmul: e.tensor_tensor(
                            out=tmul, in0=g_, in1=eq2, op=ALU.mult),
                        lambda e, s2=s2, tmul=tmul: e.reduce_sum(s2, tmul, axis=AX.X),
                        lambda e, rec=rec, s1=s1, s2=s2: e.tensor_tensor(
                            out=rec, in0=s1, in1=s2, op=ALU.add),
                        lambda e, rec=rec: e.tensor_scalar(
                            out=rec, in0=rec, scalar1=1e-20, scalar2=None, op0=ALU.add),
                        lambda e, rec=rec: e.reciprocal(out=rec, in_=rec),
                        lambda e, w1=w1, s1=s1, rec=rec: e.tensor_tensor(
                            out=w1, in0=s1, in1=rec, op=ALU.mult),
                        lambda e, w2=w2, s2=s2, rec=rec: e.tensor_tensor(
                            out=w2, in0=s2, in1=rec, op=ALU.mult),
                        lambda e, eq1=eq1, w1=w1: e.tensor_scalar(
                            out=eq1, in0=eq1, scalar1=w1, scalar2=None, op0=ALU.mult),
                        lambda e, eq1=eq1, eq2=eq2, w2=w2: e.scalar_tensor_tensor(
                            out=eq1, in0=eq2, scalar=w2, in1=eq1, op0=ALU.mult,
                            op1=ALU.add),
                    ]
                    tv = (te1[0], te1[1])
                    p.wait("vector", te2_)
                    for fn in ops:
                        tv = p.op("vector", fn, waits=[tv], inc="v")
                    bi3, bk3 = banks.get()
                    tt3 = p.op("tensor", lambda e, bk=bk3, eq1=eq1: e.transpose(
                        bk[0:16, 0:128], eq1, ident_sb[:, :]), waits=[tv], inc="pe")
                    te3 = p.op("scalar", lambda e, bk=bk3, blk=blk: e.copy(
                        out=comb_sb[:, blk * 128:(blk + 1) * 128], in_=bk[0:16, 0:128]),
                        waits=[tt3], inc="a")
                    banks.set_release(bi3, te3)
                comb_tok = p.now("a")
                if hh == 0:
                    for dn, dtsr in [("sgt", sgT), ("sbt", sbT), ("tops", tops)]:
                        if dn in dumps:
                            p.dma(lambda e, dn=dn, dtsr=dtsr: e.dma_start(
                                out=dumps[dn].ap(), in_=dtsr[:, :, :]), "s_d",
                                waits=[comb_tok, p.now("v")])
                ps3.close()
                if "comb" in dumps:
                    p.dma(lambda e, hh=hh: e.dma_start(
                        out=dumps["comb"].ap()[:, hh * 1024:(hh + 1) * 1024],
                        in_=comb_sb[:, :]), "s_d", waits=[comb_tok])

                if limit == "comb":
                    p.barrier()
                    return
                hprime = sb(f"hp{hh}", [128, 8, 1024], BF16, psB)
                wdn_b = sb(f"wdnb{hh}", [128, 2, 8, 128], BF16, psB)
                slabs = {"gb": [None, None], "db": [None, None]}
                acc_last = {}
                gs_free = [None, None]

                def gu_down_pass(gu_aps, dn_aps, cbc, init_acc, init_tok,
                                 tail_cb=None):
                    for m in range(16):
                        sl = m % 2
                        tc = p.gdma(lambda e, ap=gu_aps[m], sl=sl: e.dma_start(
                            out=wgu_b[:, sl, :, :], in_=ap), f"s_wg{sl}",
                            waits=[slabs["gb"][sl]] if slabs["gb"][sl] else [])
                        tg = None
                        grp = [banks.get(), banks.get()]
                        for kc in range(16):
                            for tt in range(2):
                                tg = p.op("tensor", lambda e, bk=grp[tt][1], sl=sl, kc=kc, tt=tt: e.matmul(
                                    bk[:, :], wgu_b[:, sl, kc, :],
                                    xnh[:, hh, kc, tt * 512:(tt + 1) * 512],
                                    start=(kc == 0), stop=(kc == 15)),
                                    waits=[tc, xn_tok[tt]] if kc == 0 else [],
                                    inc="pe" if (kc == 15 and tt == 1) else None)
                        pr, gsl = m // 2, (m // 2) % 2
                        for tt in range(2):
                            bi, bk = grp[tt]
                            if m % 2 == 0:
                                tev = p.op("scalar", lambda e, bk=bk, gsl=gsl, tt=tt: e.activation(
                                    out=gsil[:, gsl, tt * 512:(tt + 1) * 512],
                                    in_=bk[:, :], func=AF.Silu, bias=zeps_sb[:, 0:1], scale=1.0),
                                    waits=[tg] +
                                    ([gs_free[gsl]] if (tt == 0 and gs_free[gsl]) else []),
                                    inc="a")
                                banks.set_release(bi, tev)
                            else:
                                tev = p.op("vector", lambda e, bk=bk, pr=pr, gsl=gsl, tt=tt: e.tensor_tensor(
                                    out=hprime[:, pr, tt * 512:(tt + 1) * 512],
                                    in0=bk[:, :],
                                    in1=gsil[:, gsl, tt * 512:(tt + 1) * 512],
                                    op=ALU.mult), waits=[tg, p.now("a")], inc="v")
                                if tt == 1:
                                    gs_free[gsl] = tev
                                if cbc is not None:
                                    tev = p.op("vector", lambda e, pr=pr, tt=tt: e.tensor_tensor(
                                        out=hprime[:, pr, tt * 512:(tt + 1) * 512],
                                        in0=hprime[:, pr, tt * 512:(tt + 1) * 512],
                                        in1=cb_sb[:, tt * 512:(tt + 1) * 512],
                                        op=ALU.mult), waits=[cbc, tev], inc="v")
                                banks.set_release(bi, tev)
                        slabs["gb"][sl] = tg
                    hp_tok = p.now("v")
                    gs_tok = p.now("a")
                    for m in range(16):
                        sl = m % 2
                        tc = p.gdma(lambda e, ap=dn_aps[m], sl=sl: e.dma_start(
                            out=wdn_b[:, sl, :, :], in_=ap), f"s_wd{sl}",
                            waits=[slabs["db"][sl]] if slabs["db"][sl] else [])
                        tg = None
                        grp = [banks.get(), banks.get()]
                        for kc in range(8):
                            for tt in range(2):
                                tg = p.op("tensor", lambda e, bk=grp[tt][1], sl=sl, kc=kc, tt=tt: e.matmul(
                                    bk[:, :], wdn_b[:, sl, kc, :],
                                    hprime[:, kc, tt * 512:(tt + 1) * 512],
                                    start=(kc == 0), stop=(kc == 7)),
                                    waits=[tc, hp_tok, gs_tok] if kc == 0 else [],
                                    inc="pe" if (kc == 7 and tt == 1) else None)
                        add_toks = []
                        for tt in range(2):
                            bi, bk = grp[tt]
                            prev_acc = acc_last.get((m, tt))
                            if init_acc:
                                tev = p.op("scalar", lambda e, bk=bk, m=m, tt=tt: e.copy(
                                    out=acc[:, m, tt * 512:(tt + 1) * 512], in_=bk[:, :]),
                                    waits=[tg], inc="a")
                            else:
                                tev = p.op("vector", lambda e, bk=bk, m=m, tt=tt: e.tensor_tensor(
                                    out=acc[:, m, tt * 512:(tt + 1) * 512],
                                    in0=acc[:, m, tt * 512:(tt + 1) * 512],
                                    in1=bk[:, :], op=ALU.add),
                                    waits=[tg, init_tok] +
                                    ([prev_acc] if prev_acc else []), inc="v")
                            acc_last[(m, tt)] = tev
                            add_toks.append(tev)
                            banks.set_release(bi, tev)
                        if tail_cb is not None:
                            tail_cb(m, add_toks)
                        slabs["db"][sl] = tg
                    return p.now("a"), p.now("v")

                sh_gu = []
                for pr in range(8):
                    sh_gu += [dr["sgw"].ap()[pr], dr["suw"].ap()[pr]]
                sh_dn = [dr["sdw"].ap()[m] for m in range(16)]
                init_tok, _ = gu_down_pass(sh_gu, sh_dn, None, True, None)

                g_toks = []
                gslot_free = [None, None]

                def g_tail(m, add_toks, hh=hh):
                    sl = m % 2
                    tgs = p.op("scalar", lambda e, m=m, sl=sl: e.activation(
                        out=gsil[:, sl, :], in_=acc[:, m, :], func=AF.Silu,
                        bias=zeps_sb[:, 0:1], scale=1.0),
                        waits=add_toks +
                        ([gslot_free[sl]] if gslot_free[sl] else []), inc="a")
                    tdm = p.dma(lambda e, m=m, hh=hh, sl=sl: e.dma_start(
                        out=bass.AP(dr["g_s"], m * 128 * 2048 + hh * 1024,
                                    [[2048, 128], [1, 1024]]),
                        in_=gsil[:, sl, :]), f"s_g{sl}", waits=[tgs])
                    gslot_free[sl] = tdm
                    g_toks.append(tdm)

                for ex in range(E):
                    cb_tok = None
                    for tt in range(2):
                        bi, bk = banks.get()
                        tmm = p.op("tensor", lambda e, bk=bk, ex=ex, tt=tt: e.matmul(
                            bk[:, :], onehot_sb[:, ex * 128:(ex + 1) * 128],
                            comb_sb[:, tt * 512:(tt + 1) * 512],
                            start=True, stop=True),
                            waits=[comb_tok, oh_tok], inc="pe")
                        cb_tok = p.op("scalar", lambda e, bk=bk, tt=tt: e.copy(
                            out=cb_sb[:, tt * 512:(tt + 1) * 512], in_=bk[:, :]),
                            waits=[tmm, p.now("v")], inc="a")
                        banks.set_release(bi, cb_tok)
                    eg = []
                    for pr in range(8):
                        eg += [dr["egu"].ap()[ex, pr], dr["egu"].ap()[ex, 8 + pr]]
                    ed = [dr["edn"].ap()[ex, m] for m in range(16)]
                    _, _ = gu_down_pass(eg, ed, cb_tok, False, init_tok,
                                        tail_cb=g_tail if ex == E - 1 else None)
                acc_tok_a, acc_tok_v = p.now("a"), p.now("v")
                if "acc" in dumps:
                    p.dma(lambda e, hh=hh: e.dma_start(
                        out=dumps["acc"].ap()[:, :, hh * 1024:(hh + 1) * 1024],
                        in_=acc[:, :, :]), "s_d", waits=[acc_tok_a, acc_tok_v])

                p.barrier(exclude=("gpsimd",))
                psB.close()
                if limit == "acc":
                    return

                z_sb = sb(f"z{hh}", [128, 16, 1024], BF16, ph)
                psC = ExitStack()
                sb(f"shim_psC{hh}", [128, 40], F32, psC)
                q_sb = sb(f"q{hh}", [128, 16, 1024], BF16, psC)
                for m in range(16):
                    sl = m % 2
                    tc = p.gdma(lambda e, m=m, sl=sl: e.dma_start(
                        out=wgu_b[:, sl, :, :], in_=dr["wq"].ap()[m]), f"s_wg{sl}",
                        waits=[slabs["gb"][sl]] if slabs["gb"][sl] else [])
                    tg = None
                    grp = [banks.get(), banks.get()]
                    for kc in range(16):
                        for tt in range(2):
                            tg = p.op("tensor", lambda e, bk=grp[tt][1], sl=sl, kc=kc, tt=tt: e.matmul(
                                bk[:, :], wgu_b[:, sl, kc, :],
                                xnh[:, hh, kc, tt * 512:(tt + 1) * 512],
                                start=(kc == 0), stop=(kc == 15)),
                                waits=[tc, xn_tok[tt]] if kc == 0 else [],
                                inc="pe" if (kc == 15 and tt == 1) else None)
                    for tt in range(2):
                        bi, bk = grp[tt]
                        tev = p.op("scalar", lambda e, bk=bk, m=m, tt=tt: e.activation(
                            out=q_sb[:, m, tt * 512:(tt + 1) * 512], in_=bk[:, :],
                            func=AF.Copy, bias=0.0, scale=QSCALE), waits=[tg], inc="a")
                        banks.set_release(bi, tev)
                    slabs["gb"][sl] = tg
                q_tok = p.now("a")
                if "q" in dumps:
                    p.dma(lambda e, hh=hh: e.dma_start(
                        out=dumps["q"].ap()[:, :, hh * 1024:(hh + 1) * 1024],
                        in_=q_sb[:, :, :]), "s_d", waits=[q_tok])
                if limit == "q":
                    p.barrier()
                    psC.close()
                    return

                k_sb = sb(f"k{hh}", [128, 2, 8, 512], BF16, psC)
                v_sb = sb(f"v{hh}", [128, 2, 32, 128], BF16, psC)
                sc2 = sb(f"sc2{hh}", [128, 2, 512], F32, psC)
                pbf = sb(f"pbf{hh}", [128, 3, 512], BF16, psC)
                rbb = sb(f"rbb{hh}", [128, 512], F32, psC)
                alibi_sb = sb(f"alibi{hh}", [128, 512], F32, psC)
                masks_sb = sb(f"masks{hh}", [128, 8, 512], BF16, psC)
                slbc_sb = sb(f"slbc{hh}", [128, 2048], BF16, psC)
                negqbf_sb = sb(f"negqbf{hh}", [128, 2048], BF16, psC)
                for nm, dst in [("alibi", alibi_sb), ("masks", masks_sb),
                                ("slopes_mat", slbc_sb), ("negq_bf", negqbf_sb)]:
                    p.dma(lambda e, a=dr[nm], d=dst: e.dma_start(
                        out=d[tuple(slice(None) for _ in d.shape)], in_=a.ap()),
                        f"s_ac{hh}")
                atc = p.now(f"s_ac{hh}")

                kv_free = [None, None]
                sc2_free = [None, None]
                pbf_free = [None, None, None]
                sp_toks = [p.now(f"s_spk{i}") for i in range(2)] + \
                    [p.now(f"s_spv{i}") for i in range(8)]
                for h in range(16):
                    sl = h % 2
                    w_ = ([kv_free[sl]] if kv_free[sl] else []) + sp_toks
                    tkh = p.dma(lambda e, h=h, sl=sl: e.dma_start(
                        out=k_sb[:, sl, :, :],
                        in_=bass.AP(dr["k_s"], h * 524288,
                                    [[512, 128], [65536, 8], [1, 512]])),
                        f"s_k{sl}", waits=w_)
                    for vq in range(4):
                        tvh = p.dma(lambda e, h=h, sl=sl, vq=vq: e.dma_start(
                            out=v_sb[:, sl, vq * 8:(vq + 1) * 8, :],
                            in_=bass.AP(dr["v_s"], h * 524288 + vq * 8 * 16384,
                                        [[128, 128], [16384, 8], [1, 128]])),
                            f"s_v{sl}", waits=w_)
                    last_tz = None
                    for Tl in range(2):
                        T = 2 * hh + Tl
                        nkc = 8 * (T + 1)
                        bi_z, bk_z = banks.get(hold=True)
                        bi_sm, bk_sm = banks.get(hold=True)
                        DEPTH = 4
                        sgrp = {}

                        def issue_S(kc, h=h, sl=sl, Tl=Tl, T=T):
                            bi_s, bk_s = banks.get(hold=True)
                            p.op("tensor", lambda e, bk=bk_s, h=h, sl=sl, kc=kc, Tl=Tl: e.matmul(
                                bk[:, :],
                                k_sb[:, sl, kc // 4, (kc % 4) * 128:(kc % 4 + 1) * 128],
                                q_sb[:, h, Tl * 512:(Tl + 1) * 512],
                                start=True, stop=False),
                                waits=[tkh, q_tok])
                            t2 = p.op("tensor", lambda e, bk=bk_s, h=h, T=T: e.matmul(
                                bk[:, :], slbc_sb[:, h * 128:(h + 1) * 128],
                                negqbf_sb[:, T * 512:(T + 1) * 512],
                                start=False, stop=True), waits=[atc], inc="pe")
                            sgrp[kc] = (bi_s, bk_s, t2)

                        for j in range(min(DEPTH, nkc)):
                            issue_S(j)
                        tav = None
                        for kc in range(nkc):
                            bi_s, bk_s, t2 = sgrp.pop(kc)
                            si = kc % 2
                            tcl = p.op("vector", lambda e, bk=bk_s, si=si: e.tensor_scalar(
                                out=sc2[:, si, :], in0=bk[:, :], scalar1=1024.0,
                                scalar2=-30000.0, op0=ALU.min, op1=ALU.max),
                                waits=[t2] + ([sc2_free[si]] if sc2_free[si] else []),
                                inc="v")
                            banks.set_release(bi_s, tcl)
                            tb2 = tcl
                            if kc >= 8 * T:
                                tb2 = p.op("vector", lambda e, si=si, r=kc - 8 * T: e.tensor_tensor(
                                    out=sc2[:, si, :], in0=sc2[:, si, :],
                                    in1=masks_sb[:, r, :], op=ALU.add),
                                    waits=[tb2, atc], inc="v")
                            pi = kc % 3
                            texp = p.op("scalar", lambda e, si=si, pi=pi, h=h, kc=kc: e.activation(
                                out=pbf[:, pi, :], in_=sc2[:, si, :], func=AF.Exp,
                                bias=alibi_sb[:, h * 32 + kc:h * 32 + kc + 1],
                                scale=1.0),
                                waits=[tb2, atc] +
                                ([pbf_free[pi]] if pbf_free[pi] else []),
                                inc="a")
                            sc2_free[si] = texp
                            p.op("tensor", lambda e, bk=bk_sm, pi=pi, kc=kc, nkc=nkc: e.matmul(
                                bk[:, :], onesm_sb[:, :], pbf[:, pi, :],
                                start=(kc == 0), stop=(kc == nkc - 1)), waits=[texp])
                            tav = p.op("tensor", lambda e, bk=bk_z, pi=pi, sl=sl, kc=kc, nkc=nkc: e.matmul(
                                bk[:, :], v_sb[:, sl, kc, :], pbf[:, pi, :],
                                start=(kc == 0), stop=(kc == nkc - 1)),
                                waits=[tvh], inc="pe")
                            pbf_free[pi] = tav
                            if kc + DEPTH < nkc:
                                issue_S(kc + DEPTH)
                        trc = p.op("vector", lambda e, bk=bk_sm: e.reciprocal(
                            out=rbb[:, :], in_=bk[:, :]), waits=[tav], inc="v")
                        banks.set_release(bi_sm, trc)
                        tz = p.op("vector", lambda e, bk=bk_z, h=h, Tl=Tl: e.tensor_tensor(
                            out=z_sb[:, h, Tl * 512:(Tl + 1) * 512], in0=bk[:, :],
                            in1=rbb[:, :], op=ALU.mult), waits=[trc], inc="v")
                        banks.set_release(bi_z, tz)
                        last_tz = tz
                    kv_free[sl] = last_tz
                z_tok = p.now("v")
                if "z" in dumps:
                    p.dma(lambda e, hh=hh: e.dma_start(
                        out=dumps["z"].ap()[:, :, hh * 1024:(hh + 1) * 1024],
                        in_=z_sb[:, :, :]), "s_d", waits=[z_tok])
                if limit == "z":
                    p.barrier()
                    psC.close()
                    return

                tcs2 = {}
                for m in (0, 1):
                    tcs2[m] = p.gdma(lambda e, m=m: e.dma_start(
                        out=wgu_b[:, m, :, :], in_=dr["wo"].ap()[m]),
                        f"s_wg{m}",
                        waits=[slabs["gb"][m]] if slabs["gb"][m] else [])
                p.barrier(exclude=("gpsimd",))
                psC.close()
                g_rl = sb(f"grl{hh}", [128, 4, 512], BF16, ph)
                x_rl = sb(f"xrl{hh}", [128, 2, 512], F32, ph)
                o_sb = sb(f"osb{hh}", [128, 4, 512], F32, ph)
                grl_free = [None] * 4
                xrl_free = [None] * 2
                osb_free = [None] * 4
                drive2 = {}
                if hh == 0:
                    xt2f = sb("xt2f", [128, 16, 512], F32, ph)
                    xb2f = sb("xb2f", [128, 1, 16, 512], BF16, ph)
                    stats2f = sb("stats2f", [1, 3, 512], F32, ph)
                    chs2f = sb("chs2f", [128, 2, 512], F32, ph)
                    gwf2f = sb("gwf2f", [128, 16, 16], F32, ph)
                    rbx2f = sb("rbx2f", [16, 512], F32, ph)
                    tgw2 = p.dma(lambda e: e.dma_start(
                        out=gwf2f[:, :, :], in_=dr["gwt"].ap()), "s_gw2")
                    t2b = make_2a(1, xt2f, xb2f, stats2f, chs2f, (0, 0),
                                  (gwf2f, rbx2f, [tgw2]))
                    xtf = {"free": []}
                    t2b[0].stage0_sync([], [])

                    def a2f(tl):
                        t = ln2a[1]["tiles"][tl]
                        if tl == 1:
                            t.xb_wait0 = list(t2b[0].xb_free)
                        t.stage_a([])

                    def b2f(tl):
                        drive_2a_b(1, tl)
                        xtf["free"] = ln2a[1]["tiles"][tl].xt_free
                    drive2 = {2: (lambda: a2f(0)), 5: (lambda: b2f(0)),
                              7: (lambda: t2b[1].stage0_sync(xtf["free"], [])),
                              9: (lambda: a2f(1)),
                              12: (lambda: b2f(1))}
                for m in range(16):
                    if m in drive2:
                        drive2[m]()
                    sl = m % 2
                    tc = tcs2.pop(m, None)
                    if tc is None:
                        tc = p.gdma(lambda e, m=m, sl=sl: e.dma_start(
                            out=wgu_b[:, sl, :, :], in_=dr["wo"].ap()[m]),
                            f"s_wg{sl}",
                            waits=[slabs["gb"][sl]] if slabs["gb"][sl] else [])
                    tg = None
                    tls = []
                    for tt in range(2):
                        ot = (m * 2 + tt) % 4
                        tgl = p.dma(lambda e, m=m, hh=hh, tt=tt, ot=ot: e.dma_start(
                            out=g_rl[:, ot, :],
                            in_=bass.AP(dr["g_s"],
                                        m * 128 * 2048 + hh * 1024 + tt * 512,
                                        [[2048, 128], [1, 512]])), f"s_gr{ot}",
                            waits=[g_toks[m]] + ([grl_free[ot]] if grl_free[ot] else []))
                        xr = ot % 2
                        txl = p.dma(lambda e, m=m, hh=hh, tt=tt, xr=xr: e.dma_start(
                            out=x_rl[:, xr, :],
                            in_=bass.AP(dr["x_own"],
                                        m * 128 * 2048 + hh * 1024 + tt * 512,
                                        [[2048, 128], [1, 512]])), f"s_xr{xr}",
                            waits=[xrl_free[xr]] if xrl_free[xr] else [])
                        tls.append((tgl, txl, ot))
                    grp = [banks.get(), banks.get()]
                    for kc in range(16):
                        for tt in range(2):
                            tg = p.op("tensor", lambda e, bk=grp[tt][1], sl=sl, kc=kc, tt=tt: e.matmul(
                                bk[:, :], wgu_b[:, sl, kc, :],
                                z_sb[:, kc, tt * 512:(tt + 1) * 512],
                                start=(kc == 0), stop=(kc == 15)),
                                waits=[tc, z_tok] if kc == 0 else [],
                                inc="pe" if (kc == 15 and tt == 1) else None)
                    for tt in range(2):
                        bi, bk = grp[tt]
                        tgl, txl, ot = tls[tt]
                        tv1 = p.op("vector", lambda e, bk=bk, ot=ot: e.tensor_tensor(
                            out=o_sb[:, ot, :], in0=bk[:, :], in1=g_rl[:, ot, :],
                            op=ALU.mult),
                            waits=[tg, tgl] + ([osb_free[ot]] if osb_free[ot] else []),
                            inc="v")
                        grl_free[ot] = tv1
                        tv2 = p.op("vector", lambda e, ot=ot, xr=ot % 2: e.tensor_tensor(
                            out=o_sb[:, ot, :], in0=o_sb[:, ot, :], in1=x_rl[:, xr, :],
                            op=ALU.add), waits=[txl, tv1], inc="v")
                        xrl_free[ot % 2] = tv2
                        banks.set_release(bi, tv1)
                        tst = p.dma(lambda e, m=m, hh=hh, tt=tt, ot=ot: e.dma_start(
                            out=bass.AP(dr["out_fm"],
                                        m * 128 * 2048 + hh * 1024 + tt * 512,
                                        [[2048, 128], [1, 512]]),
                            in_=o_sb[:, ot, :]), f"s_o{ot}", waits=[tv2])
                        osb_free[ot] = tst
                    slabs["gb"][sl] = tg
                p.barrier()

        for hh in range(2):
            emit_phase2(hh)

    for i in range(4):
        p.wait("sync", p.now(f"s_o{i}"))
    p.wait("sync", p.now("s_d"))
    p.replay()
    nc._keepalive = st
    return nc


_CACHE = {}


def kernel(**inputs):
    if "nc" not in _CACHE:
        _CACHE["nc"] = _build()
    nc = _CACHE["nc"]
    in_maps = _host_prep(inputs)
    res = run_bass_kernel_spmd(nc, in_maps, core_ids=list(range(NCORE)))
    return _assemble(res.results)


# version 21 (final)
# speedup vs baseline: 1.0968x; 1.0121x over previous
import numpy as np
import ml_dtypes
from contextlib import ExitStack

import concourse.bass as bass
import concourse.mybir as mybir
from concourse.bass_utils import run_bass_kernel_spmd

F32 = mybir.dt.float32
BF16 = mybir.dt.bfloat16
AF = mybir.ActivationFunctionType
ALU = mybir.AluOpType
AX = mybir.AxisListType
bfnp = ml_dtypes.bfloat16

B, S, D = 4, 4096, 2048
E = 16
NCORE = 8
NEG = -1.0e30
QSCALE = 1.0 / float(np.sqrt(128.0))
SLOPES = [float(2.0 ** (-(i + 1) / 2.0)) for i in range(16)]


class Prog:
    ENGINES = ("sync", "tensor", "vector", "scalar", "gpsimd")

    def __init__(self, nc, stack):
        self.nc = nc
        self.stack = stack
        self.items = {e: [] for e in self.ENGINES}
        self._sems = {}
        self.counts = {}
        self.last_wait = {e: {} for e in self.ENGINES}

    def sem(self, name):
        if name not in self._sems:
            self._sems[name] = self.stack.enter_context(self.nc.semaphore(name))
            self.counts[name] = 0
        return self._sems[name]

    def wait(self, eng, tok):
        if tok is None:
            return
        name, val = tok
        if val <= 0:
            return
        if val <= self.last_wait[eng].get(name, 0):
            return
        self.last_wait[eng][name] = val
        self.items[eng].append(("w", self.sem(name), val))

    def op(self, eng, fn, waits=(), inc=None, incby=1):
        for tok in waits:
            self.wait(eng, tok)
        if inc is not None:
            s = self.sem(inc)
            self.counts[inc] += incby
            self.items[eng].append(("o", fn, s, incby))
            return (inc, self.counts[inc])
        self.items[eng].append(("o", fn, None, 0))
        return None

    def dma(self, fn, stream, waits=()):
        return self.op("sync", fn, waits=waits, inc=stream, incby=16)

    def gdma(self, fn, stream, waits=()):
        return self.op("gpsimd", fn, waits=waits, inc=stream, incby=16)

    def now(self, name):
        return (name, self.counts.get(name, 0))

    def barrier(self, exclude=()):
        toks = [self.now(n) for n in self._sems]
        for eng in self.ENGINES:
            if eng in exclude:
                continue
            for tok in toks:
                self.wait(eng, tok)

    def replay(self):
        with self.nc.Block() as block:
            for eng in self.ENGINES:
                if not self.items[eng]:
                    continue
                items = self.items[eng]

                def run(e, items=items):
                    for it in items:
                        if it[0] == "w":
                            e.wait_ge(it[1], it[2])
                        else:
                            ins = it[1](e)
                            if it[2] is not None:
                                ins.then_inc(it[2], it[3])

                getattr(block, eng)(run)


class Banks:

    def __init__(self, prog, banks):
        self.prog = prog
        self.banks = banks
        self.release = [None] * len(banks)
        self.held = [False] * len(banks)
        self.rr = 0

    def get(self, eng="tensor", hold=False):
        for _ in range(len(self.banks)):
            i = self.rr % len(self.banks)
            self.rr += 1
            if not self.held[i]:
                break
        else:
            raise RuntimeError("all banks held")
        if self.release[i] is not None:
            self.prog.wait(eng, self.release[i])
        self.held[i] = hold
        return i, self.banks[i]

    def set_release(self, i, tok):
        self.release[i] = tok
        self.held[i] = False


def _pretile_lhsT(w_in_out, n_m, n_kc):
    a = w_in_out.reshape(n_kc, 128, n_m, 128)
    return np.ascontiguousarray(a.transpose(2, 1, 0, 3))


def _host_prep(inputs):
    x = np.ascontiguousarray(np.asarray(inputs["L0_attn_resid_pre"], dtype=np.float32))
    Wq = np.asarray(inputs["L0_attn_Wq"], dtype=np.float32)
    Wk = np.asarray(inputs["L0_attn_Wk"], dtype=np.float32)
    Wv = np.asarray(inputs["L0_attn_Wv"], dtype=np.float32)
    Wo = np.asarray(inputs["L0_attn_Wo"], dtype=np.float32)
    gw = np.asarray(inputs["L0_ffn_gate_w"], dtype=np.float32)
    ebias = np.asarray(inputs["L0_ffn_expert_bias"], dtype=np.float32)
    sgw = np.asarray(inputs["L0_ffn_shared_gate_w"], dtype=np.float32)
    suw = np.asarray(inputs["L0_ffn_shared_up_w"], dtype=np.float32)
    sdw = np.asarray(inputs["L0_ffn_shared_down_w"], dtype=np.float32)
    egu = np.asarray(inputs["L0_ffn_experts_gate_up"], dtype=np.float32)
    edn = np.asarray(inputs["L0_ffn_experts_down"], dtype=np.float32)

    sh = {}
    sh["wq"] = _pretile_lhsT(np.ascontiguousarray(Wq.T), 16, 16)
    sh["wk"] = _pretile_lhsT(np.ascontiguousarray(Wk.T), 16, 16)
    sh["wv"] = np.ascontiguousarray(Wv.T.reshape(16, 128, 2048))
    sh["wo"] = _pretile_lhsT(np.ascontiguousarray(Wo.T), 16, 16)
    sh["sgw"] = _pretile_lhsT(sgw, 8, 16)
    sh["suw"] = _pretile_lhsT(suw, 8, 16)
    sh["sdw"] = _pretile_lhsT(sdw, 16, 8)
    sh["egu"] = np.ascontiguousarray(
        egu.reshape(E, 16, 128, 16, 128).transpose(0, 3, 2, 1, 4))
    sh["edn"] = np.ascontiguousarray(
        edn.reshape(E, 8, 128, 16, 128).transpose(0, 3, 2, 1, 4))
    sh["gwt"] = np.ascontiguousarray(gw.reshape(16, 128, 16).transpose(1, 0, 2))
    sh["ebias"] = np.ascontiguousarray(ebias.reshape(16, 1))

    oh = np.zeros((16, 2048), dtype=np.float32)
    for e in range(16):
        oh[e, e * 128:(e + 1) * 128] = 1.0
    sh["onehot"] = oh
    jj = np.arange(128)
    al = np.zeros((128, 512), dtype=np.float32)
    for h in range(16):
        for kc in range(32):
            al[:, h * 32 + kc] = np.float32(SLOPES[h]) * (128.0 * kc + jj)
    sh["alibi"] = al
    sh["ident"] = np.eye(128, dtype=np.float32)
    sh["ones_row"] = np.ones((1, 128), dtype=np.float32)
    ze = np.zeros((128, 2), dtype=np.float32); ze[:, 1] = 1e-5
    sh["zeps"] = ze
    sm = np.zeros((128, 2048), dtype=np.float32)
    sm[0, :] = np.repeat(np.array(SLOPES, dtype=np.float32), 128)
    sh["slopes_mat"] = sm.astype(bfnp)
    sh["ones_mat"] = np.ones((128, 128), dtype=bfnp)
    sh["ones_col"] = np.ones((128, 1), dtype=bfnp)

    in_maps = []
    ii = np.arange(128)
    for c in range(NCORE):
        b, par = c // 2, c % 2
        m = dict(sh)
        xb = x[b]
        m["x_fm"] = np.ascontiguousarray(xb.T).reshape(16, 128, 4096)
        xo = np.ascontiguousarray(xb.reshape(16, 2, 128, 2048)[:, par])
        m["x_own"] = np.ascontiguousarray(
            xo.transpose(2, 0, 1).reshape(16, 128, 2048))
        msk = np.zeros((128, 8, 4, 128), dtype=np.float32)
        for r in range(8):
            for v in range(4):
                Dv = 2 * v + par - r
                if Dv >= 1:
                    continue
                elif Dv == 0:
                    msk[:, r, v, :] = np.where(jj[:, None] <= ii[None, :], 0.0, NEG)
                else:
                    msk[:, r, v, :] = NEG
        m["masks"] = msk.reshape(128, 8, 512).astype(bfnp)
        nq = np.zeros((1, 2048), dtype=np.float32)
        for T in range(4):
            for v in range(4):
                u = 4 * T + v
                nq[0, T * 512 + v * 128:T * 512 + (v + 1) * 128] = -(
                    256.0 * u + 128.0 * par + ii)
        nq2 = np.zeros((128, 2048), dtype=np.float32)
        nq2[0] = nq[0]
        m["negq_bf"] = nq2.astype(bfnp)
        in_maps.append(m)
    return in_maps


def _assemble(results):
    out = np.empty((B, S, D), dtype=np.float32)
    for c in range(NCORE):
        b, par = c // 2, c % 2
        o = results[c]["out_fm"]
        arr = o.reshape(16, 128, 16, 128).transpose(2, 3, 0, 1)
        out[b].reshape(16, 2, 128, 2048)[:, par] = arr.reshape(16, 128, 2048)
    return out


def _build(dump=(), phases=("p1", "p2"), limit=None):
    nc = bass.Bass("TRN2", target_bir_lowering=False)
    st = ExitStack()
    p = Prog(nc, st)

    dr = {}
    for nm, shp, dt in [
        ("x_fm", [16, 128, 4096], F32), ("x_own", [16, 128, 2048], F32),
        ("wq", [16, 128, 16, 128], F32), ("wk", [16, 128, 16, 128], F32),
        ("wv", [16, 128, 2048], F32), ("wo", [16, 128, 16, 128], F32),
        ("sgw", [8, 128, 16, 128], F32), ("suw", [8, 128, 16, 128], F32),
        ("sdw", [16, 128, 8, 128], F32),
        ("egu", [E, 16, 128, 16, 128], F32), ("edn", [E, 16, 128, 8, 128], F32),
        ("gwt", [128, 16, 16], F32), ("ebias", [16, 1], F32),
        ("onehot", [16, 2048], F32),
        ("alibi", [128, 512], F32), ("ident", [128, 128], F32),
        ("ones_row", [1, 128], F32), ("ones_col", [128, 1], BF16),
        ("zeps", [128, 2], F32), ("slopes_mat", [128, 2048], BF16),
        ("ones_mat", [128, 128], BF16), ("negq_bf", [128, 2048], BF16),
        ("masks", [128, 8, 512], BF16),
    ]:
        dr[nm] = nc.dram_tensor(nm, shp, dt, kind="ExternalInput")
    dr["out_fm"] = nc.dram_tensor("out_fm", [16, 128, 2048], F32, kind="ExternalOutput")
    dr["k_s"] = nc.dram_tensor("k_s", [16, 8, 128, 512], BF16)
    dr["v_s"] = nc.dram_tensor("v_s", [16, 32, 128, 128], BF16)
    dr["g_s"] = nc.dram_tensor("g_s", [16, 128, 2048], BF16)
    dumps = {}
    for nm in dump:
        shp = {"k_s": [16, 8, 128, 512], "v_s": [16, 32, 128, 128],
               "g_s": [16, 128, 2048], "logits": [16, 2048],
               "xn": [16, 128, 2048], "q": [128, 16, 2048], "z": [128, 16, 2048],
               "comb": [16, 2048], "acc": [128, 16, 2048],
               "sgt": [128, 8, 16], "sbt": [128, 8, 16],
               "tops": [128, 8, 8]}[nm]
        dt = {"logits": F32, "acc": F32, "comb": F32, "sgt": F32, "sbt": F32, "tops": F32}.get(nm, BF16)
        dumps[nm] = nc.dram_tensor("dump_" + nm, shp, dt, kind="ExternalOutput")

    SB = {}

    def sb(name, shape, dt, stack=st):
        SB[name] = stack.enter_context(nc.sbuf_tensor(name, shape, dt))
        return SB[name]

    ebias_sb = sb("ebias_sb", [16, 1], F32)
    ident_sb = sb("ident_sb", [128, 128], F32)
    ones_row = sb("onesr_sb", [1, 128], F32)
    ones_col = sb("onesc_sb", [128, 1], BF16)
    zeps_sb = sb("zeps_sb", [128, 2], F32)
    onesm_sb = sb("onesm_sb", [128, 128], BF16)
    xnh = sb("xnh", [128, 2, 16, 1024], BF16)
    logit = sb("logitg", [16, 2, 1024], F32)
    qprev_pe = [None, None]

    PS = [st.enter_context(nc.psum_tensor(f"bank{i}", [128, 512], F32)) for i in range(8)]
    banks = Banks(p, PS)

    for nm, dst in [("ident", ident_sb), ("ones_row", ones_row),
                    ("ones_col", ones_col), ("ebias", ebias_sb),
                    ("zeps", zeps_sb), ("ones_mat", onesm_sb)]:
        p.dma(lambda e, a=dr[nm], d=dst: e.dma_start(
            out=d[tuple(slice(None) for _ in d.shape)], in_=a.ap()), "s_c")
    c_tok = p.now("s_c")

    p.barrier()

    class LNTile:

        def __init__(self, xt, slot, xb, stats, chs, loch, xn_dst_slices,
                     x_src_ap, router=False, xn_wait=(), loch_wait=()):
            self.xt, self.slot, self.xb = xt, slot, xb
            self.stats, self.chs, self.loch = stats, chs, loch
            self.xn_dst = xn_dst_slices
            self.x_src = x_src_ap
            self.router = router
            self.xn_wait = list(xn_wait)
            self.loch_wait = list(loch_wait)
            self.rt_tok = None

        def stage0(self, xb_wait):
            self.txb = p.gdma(lambda e: e.dma_start(
                out=self.xb[:, self.slot, :, :], in_=self.x_src),
                f"s_xb{self.slot}", waits=xb_wait)

        def stage0_sync(self, xt_wait, xb_wait):
            self.sync_mode = True
            self.tx = p.dma(lambda e: e.dma_start(
                out=self.xt[:, :, :], in_=self.x_src), "s_x", waits=xt_wait)
            self.xb_wait0 = list(xb_wait)

        def stage_a(self, xt_wait):
            xb, slot = self.xb, self.slot
            self.pe_prev = p.now("pe")
            self.v_prev = p.now("v")
            if getattr(self, "sync_mode", False):
                self.txb = p.op("scalar", lambda e: e.copy(
                    out=xb[:, slot, :, :], in_=self.xt[:, :, :]),
                    waits=[self.tx] + self.xb_wait0, inc="a")
            else:
                self.tx = p.dma(lambda e: e.dma_start(
                    out=self.xt[:, :, :], in_=self.x_src), "s_x", waits=xt_wait)
            self.bi_s, self.bk_s = banks.get(hold=True)
            ts_sum = None
            for kc in range(16):
                ts_sum = p.op("tensor", lambda e, bk=self.bk_s, kc=kc: e.matmul(
                    bk[0:1, :], ones_col[:, :], xb[:, slot, kc, :],
                    start=(kc == 0), stop=(kc == 15)),
                    waits=[self.txb, c_tok] if kc == 0 else [],
                    inc="pe" if kc == 15 else None)
            self.ts_sum = ts_sum
            self.tsq = p.op("scalar", lambda e: e.activation(
                out=xb[:, slot, :, :], in_=xb[:, slot, :, :], func=AF.Square,
                bias=zeps_sb[:, 0:1], scale=1.0),
                waits=[self.txb, ts_sum, self.pe_prev], inc="a")

        def stage_b(self):
            xb, slot, stats, chs = self.xb, self.slot, self.stats, self.chs
            xt, loch, router = self.xt, self.loch, self.router
            xn_dst_slices, tx = self.xn_dst, self.tx
            bi_q, bk_q = banks.get()
            ts_sq = None
            for kc in range(16):
                ts_sq = p.op("tensor", lambda e, bk=bk_q, kc=kc: e.matmul(
                    bk[0:1, :], ones_col[:, :], xb[:, slot, kc, :],
                    start=(kc == 0), stop=(kc == 15)),
                    waits=[self.tsq] if kc == 0 else [],
                    inc="pe" if kc == 15 else None)
            tm = p.op("scalar", lambda e, bk=self.bk_s: e.activation(
                out=stats[:, 0, :], in_=bk[0:1, :], func=AF.Copy,
                scale=1.0 / 2048.0, bias=0.0),
                waits=[self.ts_sum, self.v_prev], inc="a")
            te2 = p.op("scalar", lambda e, bk=bk_q: e.activation(
                out=stats[:, 1, :], in_=bk[0:1, :], func=AF.Copy,
                scale=1.0 / 2048.0, bias=0.0), waits=[ts_sq], inc="a")
            banks.set_release(self.bi_s, tm)
            banks.set_release(bi_q, te2)
            tv1 = p.op("vector", lambda e: e.tensor_tensor(
                out=stats[:, 2, :], in0=stats[:, 0, :], in1=stats[:, 0, :],
                op=ALU.mult), waits=[tm], inc="v")
            tv2 = p.op("vector", lambda e: e.tensor_tensor(
                out=stats[:, 2, :], in0=stats[:, 1, :], in1=stats[:, 2, :],
                op=ALU.subtract), waits=[te2, tv1], inc="v")
            tsq2 = p.op("scalar", lambda e: e.activation(
                out=stats[:, 1, :], in_=stats[:, 2, :], func=AF.Sqrt,
                bias=zeps_sb[0:1, 1:2], scale=1.0), waits=[tv2], inc="a")
            trr = p.op("vector", lambda e: e.reciprocal(
                out=stats[:, 1, :], in_=stats[:, 1, :]), waits=[tsq2], inc="v")
            bi_m, bk_m = banks.get(hold=True)
            tbm = p.op("tensor", lambda e, bk=bk_m: e.matmul(
                bk[:, :], ones_row[:, :], stats[:, 0, :], start=True, stop=True),
                waits=[tm], inc="pe")
            bi_r, bk_r = banks.get(hold=True)
            tbr = p.op("tensor", lambda e, bk=bk_r: e.matmul(
                bk[:, :], ones_row[:, :], stats[:, 1, :], start=True, stop=True),
                waits=[trr], inc="pe")
            if router:
                gw_f, rbx, logit_dst, gw_ld = self.router_ctx
                bi_rt, bk_rt = banks.get(hold=True)
                bi_rb, bk_rb = banks.get()
                trb = p.op("tensor", lambda e, bk=bk_rb: e.matmul(
                    bk[0:16, :], ones_row[0:1, 0:16], stats[:, 1, :],
                    start=True, stop=True), waits=[trr], inc="pe")
                trbe = p.op("scalar", lambda e, bk=bk_rb: e.copy(
                    out=rbx[:, :], in_=bk[0:16, :]), waits=[trb], inc="a")
                banks.set_release(bi_rb, trbe)
            xn_toks = []
            last_reader = None
            mm_toks = {}
            for kc in range(16):
                sl2 = kc % 2
                wk1 = ([mm_toks[kc - 2]] if kc >= 2 else
                       list(self.loch_wait)) if router else []
                td1 = p.op("vector", lambda e, kc=kc, sl2=sl2, bk=bk_m: e.tensor_tensor(
                    out=chs[:, sl2, :], in0=xt[:, kc, :], in1=bk[:, :],
                    op=ALU.subtract),
                    waits=[tbm, tx] + wk1 +
                    ([last_reader] if last_reader else []),
                    inc="v")
                td2 = p.op("vector", lambda e, kc=kc, sl2=sl2, bk=bk_r: e.tensor_tensor(
                    out=xn_dst_slices(kc), in0=chs[:, sl2, :], in1=bk[:, :],
                    op=ALU.mult), waits=[tbr, td1] + self.xn_wait, inc="v")
                xn_toks.append(td2)
                last_reader = td2
                if router:
                    mm_toks[kc] = p.op("tensor", lambda e, kc=kc, sl2=sl2, bk=bk_rt: e.matmul(
                        bk[0:16, :], gw_f[:, kc, :], chs[:, sl2, :],
                        start=(kc == 0), stop=(kc == 15)),
                        waits=[td1] + (gw_ld if kc == 0 else []), inc="pe")
            if router:
                self.rt_tok = p.op("vector", lambda e, bk=bk_rt: e.tensor_tensor(
                    out=logit_dst, in0=bk[0:16, :], in1=rbx[:, :],
                    op=ALU.mult), waits=[mm_toks[15], trbe], inc="v")
                banks.set_release(bi_rt, self.rt_tok)
                self.mm_last = [mm_toks[14], mm_toks[15]]
            else:
                self.rt_tok = None
            banks.set_release(bi_m, xn_toks[-1])
            banks.set_release(bi_r, p.now("v"))
            self.xn_toks = xn_toks
            self.xb_free = [self.tsq, ts_sq]
            self.xt_free = [p.now("v")]

    def emit_ln_tile(xt, slot, xb, stats, chs, loch, xn_dst_slices, x_src_ap,
                     xt_wait, xb_wait, router=None, xn_wait=()):
        t = LNTile(xt, slot, xb, stats, chs, loch, xn_dst_slices, x_src_ap,
                   router=router, xn_wait=xn_wait)
        t.stage0(xb_wait)
        t.stage_a(xt_wait)
        t.stage_b()
        return t.xt_free, t.xb_free, t.xn_toks, t.rt_tok

    ln2a = {}

    def make_2a(hh, xt_, xb_, stats_, chs_, slots, router_ctx):
        tiles = []
        for tl in range(2):
            src = dr["x_own"].ap()[:, :, hh * 1024 + tl * 512:
                                   hh * 1024 + (tl + 1) * 512] \
                .rearrange("c f s -> f c s")
            gw_f, rbx, gw_ld = router_ctx
            tiles.append(LNTile(
                xt_, slots[tl], xb_, stats_, chs_, None,
                lambda kc, hh=hh, tl=tl: xnh[:, hh, kc,
                                             tl * 512:(tl + 1) * 512],
                src, router=True,
                xn_wait=[qprev_pe[hh]] if qprev_pe[hh] else []))
            tiles[tl].router_ctx = (
                gw_f, rbx,
                logit[:, hh, tl * 512:(tl + 1) * 512], gw_ld)
        ln2a[hh] = {"tiles": tiles, "lg": [], "xn": []}
        return tiles

    def drive_2a_b(hh, tl):
        t = ln2a[hh]["tiles"][tl]
        if tl == 1:
            t.loch_wait = ln2a[hh]["tiles"][0].mm_last
        t.stage_b()
        ln2a[hh]["xn"].append(t.xn_toks[-1])
        ln2a[hh]["lg"].append(t.rt_tok)

    if "p1" in phases:
        with ExitStack() as ph:
            xt = sb("xt", [128, 16, 512], F32, ph)
            xb = sb("xb", [128, 2, 16, 512], BF16, ph)
            stats = sb("stats", [1, 3, 512], F32, ph)
            chs = sb("chs", [128, 2, 512], F32, ph)
            gwf = sb("gwf", [128, 16, 16], F32, ph)
            rbx1 = sb("rbx1", [16, 512], F32, ph)
            tgwl = p.dma(lambda e: e.dma_start(
                out=gwf[:, :, :], in_=dr["gwt"].ap()), "s_gw1")
            gw_tok = tgwl
            wslab_b = sb("wslab_b", [128, 2, 16, 128], BF16, ph)
            wvb = sb("wvb", [128, 4, 512], BF16, ph)
            kev = sb("kev", [128, 2, 512], BF16, ph)
            vev = sb("vev", [128, 8, 512], BF16, ph)

            xb_free = [[], []]
            kb_free = [None, None]
            vb_free = [None, None, None, None]
            kev_free = [None, None]
            vev_free = [None] * 8
            xt_state = {"free": []}

            def prep_quarter(qt):
                qs = qt % 2
                objs = []
                for tl in range(2):
                    stg = qt * 2 + tl
                    xsrc = dr["x_fm"].ap()[:, :, stg * 512:(stg + 1) * 512] \
                        .rearrange("c f s -> f c s")
                    objs.append(LNTile(
                        xt, tl, xb, stats, chs, None,
                        lambda kc, tl=tl, qs=qs: xnh[:, qs, kc,
                                                     tl * 512:(tl + 1) * 512],
                        xsrc, router=None,
                        xn_wait=[qprev_pe[qs]] if qprev_pe[qs] else []))
                return objs

            def _ln_a(t):
                t.stage_a(xt_state["free"])

            def _ln_b(t):
                t.stage_b()
                xt_state["free"] = t.xt_free

            wk_pre = {}

            def issue_wk(qt, m):
                wk_pre[(qt, m)] = p.gdma(lambda e, m=m: e.dma_start(
                    out=wslab_b[:, m % 2, :, :], in_=dr["wk"].ap()[m]),
                    f"s_wk{m % 2}",
                    waits=[kb_free[m % 2]] if kb_free[m % 2] else [])

            def kv_quarter(qt, xn_tok, drive, pre=None, post=None, mid=None):
                qs = qt % 2
                for m in (0, 1):
                    if (qt, m) not in wk_pre:
                        issue_wk(qt, m)
                if pre is not None:
                    pre()
                for m in range(16):
                    if m in drive:
                        drive[m]()
                    sl = m % 2
                    tc = wk_pre.pop((qt, m), None)
                    if tc is None:
                        tc = p.gdma(lambda e, m=m, sl=sl: e.dma_start(
                            out=wslab_b[:, sl, :, :], in_=dr["wk"].ap()[m]),
                            f"s_wk{sl}",
                            waits=[kb_free[sl]] if kb_free[sl] else [])
                    grp = [banks.get(), banks.get()]
                    tg = None
                    ktok = [None, None]
                    for kc in range(16):
                        for tt in range(2):
                            tg = p.op("tensor", lambda e, bk=grp[tt][1], sl=sl, kc=kc, tt=tt, qs=qs: e.matmul(
                                bk[:, :], wslab_b[:, sl, kc, :],
                                xnh[:, qs, kc, tt * 512:(tt + 1) * 512],
                                start=(kc == 0), stop=(kc == 15)),
                                waits=[tc, xn_tok[tt]] if kc == 0 else [],
                                inc="pe" if kc == 15 else None)
                            if kc == 15:
                                ktok[tt] = tg
                    kb_free[sl] = tg
                    for tt in range(2):
                        bi, bk = grp[tt]
                        ev = kev[:, tt, :]
                        tev = p.op("scalar", lambda e, bk=bk, ev=ev: e.copy(
                            out=ev, in_=bk[:, :]),
                            waits=[ktok[tt]] + ([kev_free[tt]] if kev_free[tt] else []),
                            inc="a")
                        banks.set_release(bi, tev)
                        seg = qt * 2 + tt
                        tdm = p.dma(lambda e, m=m, seg=seg, ev=ev: e.dma_start(
                            out=bass.AP(dr["k_s"], m * 524288 + seg * 65536,
                                        [[512, 128], [1, 512]]),
                            in_=ev), f"s_spk{tt}", waits=[tev])
                        kev_free[tt] = tdm
                for ct in range(4):
                    vgrp = []
                    for tb in range(8):
                        vgrp.append(banks.get())
                    tg = None
                    vtok = [None] * 8
                    for kc in range(16):
                        sl = kc % 4
                        tc = p.gdma(lambda e, kc=kc, ct=ct, sl=sl: e.dma_start(
                            out=wvb[:, sl, :],
                            in_=bass.AP(dr["wv"], kc * 128 * 2048 + ct * 512,
                                        [[2048, 128], [1, 512]])), f"s_wv{sl}",
                            waits=[vb_free[sl]] if vb_free[sl] else [])
                        for tb in range(8):
                            tg = p.op("tensor", lambda e, bk=vgrp[tb][1], kc=kc, tb=tb, sl=sl, qs=qs: e.matmul(
                                bk[:, :], xnh[:, qs, kc, tb * 128:(tb + 1) * 128],
                                wvb[:, sl, :], start=(kc == 0), stop=(kc == 15)),
                                waits=([tc, xn_tok[0]] if tb == 0 else
                                       ([xn_tok[1]] if (tb == 4 and kc == 0)
                                        else [])),
                                inc="pe" if (tb == 7 or kc == 15) else None)
                            if kc == 15:
                                vtok[tb] = tg
                        vb_free[sl] = tg
                    for tb in range(8):
                        bi, bk = vgrp[tb]
                        ev = vev[:, tb, :]
                        tev = p.op("scalar", lambda e, bk=bk, ev=ev: e.copy(
                            out=ev, in_=bk[:, :]),
                            waits=[vtok[tb]] + ([vev_free[tb]] if vev_free[tb] else []),
                            inc="a")
                        banks.set_release(bi, tev)
                        kb = qt * 8 + tb
                        tdm = p.dma(lambda e, kb=kb, ct=ct, ev=ev: e.dma_start(
                            out=bass.AP(dr["v_s"], 4 * ct * 524288 + kb * 16384,
                                        [[128, 128], [524288, 4], [1, 128]]),
                            in_=ev.rearrange("kk (hh dh) -> kk hh dh", hh=4)),
                            f"s_spv{tb}", waits=[tev])
                        vev_free[tb] = tdm
                    if ct == 0 and mid is not None:
                        mid()
                if post is not None:
                    post()
                qprev_pe[qs] = p.now("pe")

            cur = prep_quarter(0)
            for tl in range(2):
                cur[tl].stage0(xb_free[tl])
                _ln_a(cur[tl])
                _ln_b(cur[tl])
            xn_tok = (cur[0].xn_toks[-1], cur[1].xn_toks[-1])
            for qt in range(4):
                drive, pre, nxt, post, mid = {}, None, None, None, None
                if qt + 1 < 4:
                    nxt = prep_quarter(qt + 1)

                    def pre(nxt=nxt, cur=cur):
                        nxt[0].stage0(cur[0].xb_free)
                        nxt[1].stage0(cur[1].xb_free)
                    drive = {4: (lambda t=nxt[0]: _ln_a(t)),
                             8: (lambda t=nxt[0]: _ln_b(t)),
                             11: (lambda t=nxt[1]: _ln_a(t)),
                             14: (lambda t=nxt[1]: _ln_b(t))}
                    mid = (lambda qt=qt: (issue_wk(qt + 1, 0),
                                          issue_wk(qt + 1, 1)))
                else:
                    t2a = make_2a(0, xt, xb, stats, chs, (0, 1),
                                  (gwf, rbx1, [gw_tok]))

                    def pre(t2a=t2a, cur=cur):
                        t2a[0].stage0(cur[0].xb_free)
                        t2a[1].stage0(cur[1].xb_free)

                    def a2(tl):
                        ln2a[0]["tiles"][tl].stage_a(xt_state["free"])

                    def b2(tl):
                        drive_2a_b(0, tl)
                        xt_state["free"] = ln2a[0]["tiles"][tl].xt_free
                    drive = {4: (lambda: a2(0)), 8: (lambda: b2(0)),
                             11: (lambda: a2(1)), 14: (lambda: b2(1))}
                kv_quarter(qt, xn_tok, drive, pre, post, mid)
                if nxt is not None:
                    cur = nxt
                    xn_tok = (nxt[0].xn_toks[-1], nxt[1].xn_toks[-1])
            p.barrier()
        if "k_s" in dumps:
            p.dma(lambda e: e.dma_start(out=dumps["k_s"].ap(), in_=dr["k_s"].ap()), "s_d")
        if "v_s" in dumps:
            p.dma(lambda e: e.dma_start(out=dumps["v_s"].ap(), in_=dr["v_s"].ap()), "s_d")

    if "p2" in phases:
        def emit_phase2(hh):
            with ExitStack() as ph:
                sb(f"shim_ph{hh}", [128, 8 + 64 * hh], F32, ph)
                wgu_b = sb(f"wgub{hh}", [128, 2, 16, 128], BF16, ph)

                xn_tok = tuple(ln2a[hh]["xn"])
                lg_tok = ln2a[hh]["lg"]
                if "xn" in dumps:
                    for kc in range(16):
                        p.dma(lambda e, hh=hh, kc=kc: e.dma_start(
                            out=dumps["xn"].ap()[kc, :, hh * 1024:(hh + 1) * 1024],
                            in_=xnh[:, hh, kc, :]), "s_d", waits=list(xn_tok))
                if "logits" in dumps:
                    p.dma(lambda e, hh=hh: e.dma_start(
                        out=dumps["logits"].ap()[:, hh * 1024:(hh + 1) * 1024],
                        in_=logit[:, hh, :]), "s_d", waits=lg_tok)

                psB = ExitStack()
                sb(f"shim_psB{hh}", [128, 32], F32, psB)
                acc = sb(f"acc{hh}", [128, 16, 1024], F32, psB)
                gsil = sb(f"gsil{hh}", [128, 2, 1024], BF16, psB)
                comb_sb = sb(f"comb{hh}", [16, 1024], F32, psB)
                cb_sb = sb(f"cb{hh}", [128, 1024], F32, psB)
                onehot_sb = sb(f"onehot{hh}", [16, 2048], F32, psB)
                p.dma(lambda e: e.dma_start(
                    out=onehot_sb[:, :], in_=dr["onehot"].ap()), f"s_oh{hh}")
                oh_tok = p.now(f"s_oh{hh}")

                ps3 = ExitStack()
                sb(f"shim_ps3{hh}", [128, 24], F32, ps3)
                sg_sb = sb(f"sg{hh}", [16, 1024], F32, ps3)
                sbb_sb = sb(f"sbb{hh}", [16, 1024], F32, ps3)
                sgT = sb(f"sgT{hh}", [128, 8, 16], F32, ps3)
                sbT = sb(f"sbT{hh}", [128, 8, 16], F32, ps3)
                tops = sb(f"tops{hh}", [128, 8, 8], F32, ps3)
                tscr = sb(f"tscr{hh}", [128, 5, 16], F32, ps3)

                tsg = p.op("scalar", lambda e: e.activation(
                    out=sg_sb[:, :], in_=logit[:, hh, :], func=AF.Sigmoid,
                    bias=zeps_sb[0:16, 0:1],
                    scale=1.0), waits=lg_tok, inc="a")
                tsb = p.op("vector", lambda e: e.tensor_scalar(
                    out=sbb_sb[:, :], in0=sg_sb[:, :], scalar1=ebias_sb[:, :],
                    scalar2=None, op0=ALU.add), waits=[tsg, c_tok], inc="v")
                for blk in range(8):
                    bi1, bk1 = banks.get()
                    tt1 = p.op("tensor", lambda e, bk=bk1, blk=blk: e.transpose(
                        bk[0:128, 0:16], sg_sb[:, blk * 128:(blk + 1) * 128],
                        ident_sb[0:16, 0:16]), waits=[tsg], inc="pe")
                    te1 = p.op("scalar", lambda e, bk=bk1, blk=blk: e.copy(
                        out=sgT[:, blk, :], in_=bk[0:128, 0:16]), waits=[tt1], inc="a")
                    banks.set_release(bi1, te1)
                    bi2, bk2 = banks.get()
                    tt2 = p.op("tensor", lambda e, bk=bk2, blk=blk: e.transpose(
                        bk[0:128, 0:16], sbb_sb[:, blk * 128:(blk + 1) * 128],
                        ident_sb[0:16, 0:16]), waits=[tsb], inc="pe")
                    te2_ = p.op("scalar", lambda e, bk=bk2, blk=blk: e.copy(
                        out=sbT[:, blk, :], in_=bk[0:128, 0:16]), waits=[tt2], inc="a")
                    banks.set_release(bi2, te2_)
                    m1 = tops[:, blk, 0:1]; m2 = tops[:, blk, 1:2]
                    s1 = tops[:, blk, 2:3]; s2 = tops[:, blk, 3:4]
                    rec = tops[:, blk, 4:5]; w1 = tops[:, blk, 5:6]; w2 = tops[:, blk, 6:7]
                    eq1 = tscr[:, 0, :]; mskd = tscr[:, 1, :]; tmul = tscr[:, 2, :]
                    eq2 = tscr[:, 3, :]
                    x_ = sbT[:, blk, :]; g_ = sgT[:, blk, :]
                    ops = [
                        lambda e, x_=x_, m1=m1: e.reduce_max(m1, x_, axis=AX.X),
                        lambda e, x_=x_, m1=m1, eq1=eq1: e.tensor_scalar(
                            out=eq1, in0=x_, scalar1=m1, scalar2=None, op0=ALU.is_equal),
                        lambda e, x_=x_, eq1=eq1, mskd=mskd: e.scalar_tensor_tensor(
                            out=mskd, in0=eq1, scalar=NEG, in1=x_, op0=ALU.mult,
                            op1=ALU.add),
                        lambda e, m2=m2, mskd=mskd: e.reduce_max(m2, mskd, axis=AX.X),
                        lambda e, g_=g_, eq1=eq1, tmul=tmul: e.tensor_tensor(
                            out=tmul, in0=g_, in1=eq1, op=ALU.mult),
                        lambda e, s1=s1, tmul=tmul: e.reduce_sum(s1, tmul, axis=AX.X),
                        lambda e, m2=m2, mskd=mskd, eq2=eq2: e.tensor_scalar(
                            out=eq2, in0=mskd, scalar1=m2, scalar2=None,
                            op0=ALU.is_equal),
                        lambda e, g_=g_, eq2=eq2, tmul=tmul: e.tensor_tensor(
                            out=tmul, in0=g_, in1=eq2, op=ALU.mult),
                        lambda e, s2=s2, tmul=tmul: e.reduce_sum(s2, tmul, axis=AX.X),
                        lambda e, rec=rec, s1=s1, s2=s2: e.tensor_tensor(
                            out=rec, in0=s1, in1=s2, op=ALU.add),
                        lambda e, rec=rec: e.tensor_scalar(
                            out=rec, in0=rec, scalar1=1e-20, scalar2=None, op0=ALU.add),
                        lambda e, rec=rec: e.reciprocal(out=rec, in_=rec),
                        lambda e, w1=w1, s1=s1, rec=rec: e.tensor_tensor(
                            out=w1, in0=s1, in1=rec, op=ALU.mult),
                        lambda e, w2=w2, s2=s2, rec=rec: e.tensor_tensor(
                            out=w2, in0=s2, in1=rec, op=ALU.mult),
                        lambda e, eq1=eq1, w1=w1: e.tensor_scalar(
                            out=eq1, in0=eq1, scalar1=w1, scalar2=None, op0=ALU.mult),
                        lambda e, eq1=eq1, eq2=eq2, w2=w2: e.scalar_tensor_tensor(
                            out=eq1, in0=eq2, scalar=w2, in1=eq1, op0=ALU.mult,
                            op1=ALU.add),
                    ]
                    tv = (te1[0], te1[1])
                    p.wait("vector", te2_)
                    for fn in ops:
                        tv = p.op("vector", fn, waits=[tv], inc="v")
                    bi3, bk3 = banks.get()
                    tt3 = p.op("tensor", lambda e, bk=bk3, eq1=eq1: e.transpose(
                        bk[0:16, 0:128], eq1, ident_sb[:, :]), waits=[tv], inc="pe")
                    te3 = p.op("scalar", lambda e, bk=bk3, blk=blk: e.copy(
                        out=comb_sb[:, blk * 128:(blk + 1) * 128], in_=bk[0:16, 0:128]),
                        waits=[tt3], inc="a")
                    banks.set_release(bi3, te3)
                comb_tok = p.now("a")
                if hh == 0:
                    for dn, dtsr in [("sgt", sgT), ("sbt", sbT), ("tops", tops)]:
                        if dn in dumps:
                            p.dma(lambda e, dn=dn, dtsr=dtsr: e.dma_start(
                                out=dumps[dn].ap(), in_=dtsr[:, :, :]), "s_d",
                                waits=[comb_tok, p.now("v")])
                ps3.close()
                if "comb" in dumps:
                    p.dma(lambda e, hh=hh: e.dma_start(
                        out=dumps["comb"].ap()[:, hh * 1024:(hh + 1) * 1024],
                        in_=comb_sb[:, :]), "s_d", waits=[comb_tok])

                if limit == "comb":
                    p.barrier()
                    return
                hprime = sb(f"hp{hh}", [128, 8, 1024], BF16, psB)
                wdn_b = sb(f"wdnb{hh}", [128, 2, 8, 128], BF16, psB)
                slabs = {"gb": [None, None], "db": [None, None]}
                acc_last = {}
                gs_free = [None, None]

                def gu_down_pass(gu_aps, dn_aps, cbc, init_acc, init_tok,
                                 tail_cb=None):
                    for m in range(16):
                        sl = m % 2
                        tc = p.gdma(lambda e, ap=gu_aps[m], sl=sl: e.dma_start(
                            out=wgu_b[:, sl, :, :], in_=ap), f"s_wg{sl}",
                            waits=[slabs["gb"][sl]] if slabs["gb"][sl] else [])
                        tg = None
                        grp = [banks.get(), banks.get()]
                        for kc in range(16):
                            for tt in range(2):
                                tg = p.op("tensor", lambda e, bk=grp[tt][1], sl=sl, kc=kc, tt=tt: e.matmul(
                                    bk[:, :], wgu_b[:, sl, kc, :],
                                    xnh[:, hh, kc, tt * 512:(tt + 1) * 512],
                                    start=(kc == 0), stop=(kc == 15)),
                                    waits=[tc, xn_tok[tt]] if kc == 0 else [],
                                    inc="pe" if (kc == 15 and tt == 1) else None)
                        pr, gsl = m // 2, (m // 2) % 2
                        for tt in range(2):
                            bi, bk = grp[tt]
                            if m % 2 == 0:
                                tev = p.op("scalar", lambda e, bk=bk, gsl=gsl, tt=tt: e.activation(
                                    out=gsil[:, gsl, tt * 512:(tt + 1) * 512],
                                    in_=bk[:, :], func=AF.Silu, bias=zeps_sb[:, 0:1], scale=1.0),
                                    waits=[tg] +
                                    ([gs_free[gsl]] if (tt == 0 and gs_free[gsl]) else []),
                                    inc="a")
                                banks.set_release(bi, tev)
                            else:
                                tev = p.op("vector", lambda e, bk=bk, pr=pr, gsl=gsl, tt=tt: e.tensor_tensor(
                                    out=hprime[:, pr, tt * 512:(tt + 1) * 512],
                                    in0=bk[:, :],
                                    in1=gsil[:, gsl, tt * 512:(tt + 1) * 512],
                                    op=ALU.mult), waits=[tg, p.now("a")], inc="v")
                                if tt == 1:
                                    gs_free[gsl] = tev
                                if cbc is not None:
                                    tev = p.op("vector", lambda e, pr=pr, tt=tt: e.tensor_tensor(
                                        out=hprime[:, pr, tt * 512:(tt + 1) * 512],
                                        in0=hprime[:, pr, tt * 512:(tt + 1) * 512],
                                        in1=cb_sb[:, tt * 512:(tt + 1) * 512],
                                        op=ALU.mult), waits=[cbc, tev], inc="v")
                                banks.set_release(bi, tev)
                        slabs["gb"][sl] = tg
                    hp_tok = p.now("v")
                    gs_tok = p.now("a")
                    for m in range(16):
                        sl = m % 2
                        tc = p.gdma(lambda e, ap=dn_aps[m], sl=sl: e.dma_start(
                            out=wdn_b[:, sl, :, :], in_=ap), f"s_wd{sl}",
                            waits=[slabs["db"][sl]] if slabs["db"][sl] else [])
                        tg = None
                        grp = [banks.get(), banks.get()]
                        for kc in range(8):
                            for tt in range(2):
                                tg = p.op("tensor", lambda e, bk=grp[tt][1], sl=sl, kc=kc, tt=tt: e.matmul(
                                    bk[:, :], wdn_b[:, sl, kc, :],
                                    hprime[:, kc, tt * 512:(tt + 1) * 512],
                                    start=(kc == 0), stop=(kc == 7)),
                                    waits=[tc, hp_tok, gs_tok] if kc == 0 else [],
                                    inc="pe" if (kc == 7 and tt == 1) else None)
                        add_toks = []
                        for tt in range(2):
                            bi, bk = grp[tt]
                            prev_acc = acc_last.get((m, tt))
                            if init_acc:
                                tev = p.op("scalar", lambda e, bk=bk, m=m, tt=tt: e.copy(
                                    out=acc[:, m, tt * 512:(tt + 1) * 512], in_=bk[:, :]),
                                    waits=[tg], inc="a")
                            else:
                                tev = p.op("vector", lambda e, bk=bk, m=m, tt=tt: e.tensor_tensor(
                                    out=acc[:, m, tt * 512:(tt + 1) * 512],
                                    in0=acc[:, m, tt * 512:(tt + 1) * 512],
                                    in1=bk[:, :], op=ALU.add),
                                    waits=[tg, init_tok] +
                                    ([prev_acc] if prev_acc else []), inc="v")
                            acc_last[(m, tt)] = tev
                            add_toks.append(tev)
                            banks.set_release(bi, tev)
                        if tail_cb is not None:
                            tail_cb(m, add_toks)
                        slabs["db"][sl] = tg
                    return p.now("a"), p.now("v")

                sh_gu = []
                for pr in range(8):
                    sh_gu += [dr["sgw"].ap()[pr], dr["suw"].ap()[pr]]
                sh_dn = [dr["sdw"].ap()[m] for m in range(16)]
                init_tok, _ = gu_down_pass(sh_gu, sh_dn, None, True, None)

                g_toks = []
                gslot_free = [None, None]

                def g_tail(m, add_toks, hh=hh):
                    sl = m % 2
                    tgs = p.op("scalar", lambda e, m=m, sl=sl: e.activation(
                        out=gsil[:, sl, :], in_=acc[:, m, :], func=AF.Silu,
                        bias=zeps_sb[:, 0:1], scale=1.0),
                        waits=add_toks +
                        ([gslot_free[sl]] if gslot_free[sl] else []), inc="a")
                    tdm = p.dma(lambda e, m=m, hh=hh, sl=sl: e.dma_start(
                        out=bass.AP(dr["g_s"], m * 128 * 2048 + hh * 1024,
                                    [[2048, 128], [1, 1024]]),
                        in_=gsil[:, sl, :]), f"s_g{sl}", waits=[tgs])
                    gslot_free[sl] = tdm
                    g_toks.append(tdm)

                for ex in range(E):
                    cb_tok = None
                    for tt in range(2):
                        bi, bk = banks.get()
                        tmm = p.op("tensor", lambda e, bk=bk, ex=ex, tt=tt: e.matmul(
                            bk[:, :], onehot_sb[:, ex * 128:(ex + 1) * 128],
                            comb_sb[:, tt * 512:(tt + 1) * 512],
                            start=True, stop=True),
                            waits=[comb_tok, oh_tok], inc="pe")
                        cb_tok = p.op("scalar", lambda e, bk=bk, tt=tt: e.copy(
                            out=cb_sb[:, tt * 512:(tt + 1) * 512], in_=bk[:, :]),
                            waits=[tmm, p.now("v")], inc="a")
                        banks.set_release(bi, cb_tok)
                    eg = []
                    for pr in range(8):
                        eg += [dr["egu"].ap()[ex, pr], dr["egu"].ap()[ex, 8 + pr]]
                    ed = [dr["edn"].ap()[ex, m] for m in range(16)]
                    _, _ = gu_down_pass(eg, ed, cb_tok, False, init_tok,
                                        tail_cb=g_tail if ex == E - 1 else None)
                acc_tok_a, acc_tok_v = p.now("a"), p.now("v")
                if "acc" in dumps:
                    p.dma(lambda e, hh=hh: e.dma_start(
                        out=dumps["acc"].ap()[:, :, hh * 1024:(hh + 1) * 1024],
                        in_=acc[:, :, :]), "s_d", waits=[acc_tok_a, acc_tok_v])

                p.barrier(exclude=("gpsimd",))
                psB.close()
                if limit == "acc":
                    return

                z_sb = sb(f"z{hh}", [128, 16, 1024], BF16, ph)
                psC = ExitStack()
                sb(f"shim_psC{hh}", [128, 40], F32, psC)
                q_sb = sb(f"q{hh}", [128, 16, 1024], BF16, psC)
                for m in range(16):
                    sl = m % 2
                    tc = p.gdma(lambda e, m=m, sl=sl: e.dma_start(
                        out=wgu_b[:, sl, :, :], in_=dr["wq"].ap()[m]), f"s_wg{sl}",
                        waits=[slabs["gb"][sl]] if slabs["gb"][sl] else [])
                    tg = None
                    grp = [banks.get(), banks.get()]
                    for kc in range(16):
                        for tt in range(2):
                            tg = p.op("tensor", lambda e, bk=grp[tt][1], sl=sl, kc=kc, tt=tt: e.matmul(
                                bk[:, :], wgu_b[:, sl, kc, :],
                                xnh[:, hh, kc, tt * 512:(tt + 1) * 512],
                                start=(kc == 0), stop=(kc == 15)),
                                waits=[tc, xn_tok[tt]] if kc == 0 else [],
                                inc="pe" if (kc == 15 and tt == 1) else None)
                    for tt in range(2):
                        bi, bk = grp[tt]
                        tev = p.op("scalar", lambda e, bk=bk, m=m, tt=tt: e.activation(
                            out=q_sb[:, m, tt * 512:(tt + 1) * 512], in_=bk[:, :],
                            func=AF.Copy, bias=0.0, scale=QSCALE), waits=[tg], inc="a")
                        banks.set_release(bi, tev)
                    slabs["gb"][sl] = tg
                q_tok = p.now("a")
                if "q" in dumps:
                    p.dma(lambda e, hh=hh: e.dma_start(
                        out=dumps["q"].ap()[:, :, hh * 1024:(hh + 1) * 1024],
                        in_=q_sb[:, :, :]), "s_d", waits=[q_tok])
                if limit == "q":
                    p.barrier()
                    psC.close()
                    return

                k_sb = sb(f"k{hh}", [128, 2, 8, 512], BF16, psC)
                v_sb = sb(f"v{hh}", [128, 2, 32, 128], BF16, psC)
                sc2 = sb(f"sc2{hh}", [128, 3, 512], F32, psC)
                pbf = sb(f"pbf{hh}", [128, 4, 512], BF16, psC)
                rbb = sb(f"rbb{hh}", [128, 512], F32, psC)
                alibi_sb = sb(f"alibi{hh}", [128, 512], F32, psC)
                masks_sb = sb(f"masks{hh}", [128, 8, 512], BF16, psC)
                slbc_sb = sb(f"slbc{hh}", [128, 2048], BF16, psC)
                negqbf_sb = sb(f"negqbf{hh}", [128, 2048], BF16, psC)
                for nm, dst in [("alibi", alibi_sb), ("masks", masks_sb),
                                ("slopes_mat", slbc_sb), ("negq_bf", negqbf_sb)]:
                    p.dma(lambda e, a=dr[nm], d=dst: e.dma_start(
                        out=d[tuple(slice(None) for _ in d.shape)], in_=a.ap()),
                        f"s_ac{hh}")
                atc = p.now(f"s_ac{hh}")

                kv_free = [None, None]
                sc2_free = [None, None, None]
                pbf_free = [None, None, None, None]
                sp_toks = [p.now(f"s_spk{i}") for i in range(2)] + \
                    [p.now(f"s_spv{i}") for i in range(8)]
                for h in range(16):
                    sl = h % 2
                    w_ = ([kv_free[sl]] if kv_free[sl] else []) + sp_toks
                    tkh = p.dma(lambda e, h=h, sl=sl: e.dma_start(
                        out=k_sb[:, sl, :, :],
                        in_=bass.AP(dr["k_s"], h * 524288,
                                    [[512, 128], [65536, 8], [1, 512]])),
                        f"s_k{sl}", waits=w_)
                    for vq in range(4):
                        tvh = p.dma(lambda e, h=h, sl=sl, vq=vq: e.dma_start(
                            out=v_sb[:, sl, vq * 8:(vq + 1) * 8, :],
                            in_=bass.AP(dr["v_s"], h * 524288 + vq * 8 * 16384,
                                        [[128, 128], [16384, 8], [1, 128]])),
                            f"s_v{sl}", waits=w_)
                    last_tz = None
                    for Tl in range(2):
                        T = 2 * hh + Tl
                        nkc = 8 * (T + 1)
                        bi_z, bk_z = banks.get(hold=True)
                        bi_sm, bk_sm = banks.get(hold=True)
                        DEPTH = 4
                        sgrp = {}

                        def issue_S(kc, h=h, sl=sl, Tl=Tl, T=T):
                            bi_s, bk_s = banks.get(hold=True)
                            p.op("tensor", lambda e, bk=bk_s, h=h, sl=sl, kc=kc, Tl=Tl: e.matmul(
                                bk[:, :],
                                k_sb[:, sl, kc // 4, (kc % 4) * 128:(kc % 4 + 1) * 128],
                                q_sb[:, h, Tl * 512:(Tl + 1) * 512],
                                start=True, stop=False),
                                waits=[tkh, q_tok])
                            t2 = p.op("tensor", lambda e, bk=bk_s, h=h, T=T: e.matmul(
                                bk[:, :], slbc_sb[:, h * 128:(h + 1) * 128],
                                negqbf_sb[:, T * 512:(T + 1) * 512],
                                start=False, stop=True), waits=[atc], inc="pe")
                            sgrp[kc] = (bi_s, bk_s, t2)

                        for j in range(min(DEPTH, nkc)):
                            issue_S(j)
                        tav = None
                        for kc in range(nkc):
                            bi_s, bk_s, t2 = sgrp.pop(kc)
                            si = kc % 3
                            tcl = p.op("vector", lambda e, bk=bk_s, si=si: e.tensor_scalar(
                                out=sc2[:, si, :], in0=bk[:, :], scalar1=1024.0,
                                scalar2=-30000.0, op0=ALU.min, op1=ALU.max),
                                waits=[t2] + ([sc2_free[si]] if sc2_free[si] else []),
                                inc="v")
                            banks.set_release(bi_s, tcl)
                            tb2 = tcl
                            if kc >= 8 * T:
                                tb2 = p.op("vector", lambda e, si=si, r=kc - 8 * T: e.tensor_tensor(
                                    out=sc2[:, si, :], in0=sc2[:, si, :],
                                    in1=masks_sb[:, r, :], op=ALU.add),
                                    waits=[tb2, atc], inc="v")
                            pi = kc % 4
                            texp = p.op("scalar", lambda e, si=si, pi=pi, h=h, kc=kc: e.activation(
                                out=pbf[:, pi, :], in_=sc2[:, si, :], func=AF.Exp,
                                bias=alibi_sb[:, h * 32 + kc:h * 32 + kc + 1],
                                scale=1.0),
                                waits=[tb2, atc] +
                                ([pbf_free[pi]] if pbf_free[pi] else []),
                                inc="a")
                            sc2_free[si] = texp
                            p.op("tensor", lambda e, bk=bk_sm, pi=pi, kc=kc, nkc=nkc: e.matmul(
                                bk[:, :], onesm_sb[:, :], pbf[:, pi, :],
                                start=(kc == 0), stop=(kc == nkc - 1)), waits=[texp])
                            tav = p.op("tensor", lambda e, bk=bk_z, pi=pi, sl=sl, kc=kc, nkc=nkc: e.matmul(
                                bk[:, :], v_sb[:, sl, kc, :], pbf[:, pi, :],
                                start=(kc == 0), stop=(kc == nkc - 1)),
                                waits=[tvh], inc="pe")
                            pbf_free[pi] = tav
                            if kc + DEPTH < nkc:
                                issue_S(kc + DEPTH)
                        trc = p.op("vector", lambda e, bk=bk_sm: e.reciprocal(
                            out=rbb[:, :], in_=bk[:, :]), waits=[tav], inc="v")
                        banks.set_release(bi_sm, trc)
                        tz = p.op("vector", lambda e, bk=bk_z, h=h, Tl=Tl: e.tensor_tensor(
                            out=z_sb[:, h, Tl * 512:(Tl + 1) * 512], in0=bk[:, :],
                            in1=rbb[:, :], op=ALU.mult), waits=[trc], inc="v")
                        banks.set_release(bi_z, tz)
                        last_tz = tz
                    kv_free[sl] = last_tz
                z_tok = p.now("v")
                if "z" in dumps:
                    p.dma(lambda e, hh=hh: e.dma_start(
                        out=dumps["z"].ap()[:, :, hh * 1024:(hh + 1) * 1024],
                        in_=z_sb[:, :, :]), "s_d", waits=[z_tok])
                if limit == "z":
                    p.barrier()
                    psC.close()
                    return

                tcs2 = {}
                for m in (0, 1):
                    tcs2[m] = p.gdma(lambda e, m=m: e.dma_start(
                        out=wgu_b[:, m, :, :], in_=dr["wo"].ap()[m]),
                        f"s_wg{m}",
                        waits=[slabs["gb"][m]] if slabs["gb"][m] else [])
                p.barrier(exclude=("gpsimd",))
                psC.close()
                g_rl = sb(f"grl{hh}", [128, 4, 512], BF16, ph)
                x_rl = sb(f"xrl{hh}", [128, 2, 512], F32, ph)
                o_sb = sb(f"osb{hh}", [128, 4, 512], F32, ph)
                grl_free = [None] * 4
                xrl_free = [None] * 2
                osb_free = [None] * 4
                drive2 = {}
                if hh == 0:
                    xt2f = sb("xt2f", [128, 16, 512], F32, ph)
                    xb2f = sb("xb2f", [128, 1, 16, 512], BF16, ph)
                    stats2f = sb("stats2f", [1, 3, 512], F32, ph)
                    chs2f = sb("chs2f", [128, 2, 512], F32, ph)
                    gwf2f = sb("gwf2f", [128, 16, 16], F32, ph)
                    rbx2f = sb("rbx2f", [16, 512], F32, ph)
                    tgw2 = p.dma(lambda e: e.dma_start(
                        out=gwf2f[:, :, :], in_=dr["gwt"].ap()), "s_gw2")
                    t2b = make_2a(1, xt2f, xb2f, stats2f, chs2f, (0, 0),
                                  (gwf2f, rbx2f, [tgw2]))
                    xtf = {"free": []}
                    t2b[0].stage0_sync([], [])

                    def a2f(tl):
                        t = ln2a[1]["tiles"][tl]
                        if tl == 1:
                            t.xb_wait0 = list(t2b[0].xb_free)
                        t.stage_a([])

                    def b2f(tl):
                        drive_2a_b(1, tl)
                        xtf["free"] = ln2a[1]["tiles"][tl].xt_free
                    drive2 = {2: (lambda: a2f(0)), 5: (lambda: b2f(0)),
                              7: (lambda: t2b[1].stage0_sync(xtf["free"], [])),
                              9: (lambda: a2f(1)),
                              12: (lambda: b2f(1))}
                for m in range(16):
                    if m in drive2:
                        drive2[m]()
                    sl = m % 2
                    tc = tcs2.pop(m, None)
                    if tc is None:
                        tc = p.gdma(lambda e, m=m, sl=sl: e.dma_start(
                            out=wgu_b[:, sl, :, :], in_=dr["wo"].ap()[m]),
                            f"s_wg{sl}",
                            waits=[slabs["gb"][sl]] if slabs["gb"][sl] else [])
                    tg = None
                    tls = []
                    for tt in range(2):
                        ot = (m * 2 + tt) % 4
                        tgl = p.dma(lambda e, m=m, hh=hh, tt=tt, ot=ot: e.dma_start(
                            out=g_rl[:, ot, :],
                            in_=bass.AP(dr["g_s"],
                                        m * 128 * 2048 + hh * 1024 + tt * 512,
                                        [[2048, 128], [1, 512]])), f"s_gr{ot}",
                            waits=[g_toks[m]] + ([grl_free[ot]] if grl_free[ot] else []))
                        xr = ot % 2
                        txl = p.dma(lambda e, m=m, hh=hh, tt=tt, xr=xr: e.dma_start(
                            out=x_rl[:, xr, :],
                            in_=bass.AP(dr["x_own"],
                                        m * 128 * 2048 + hh * 1024 + tt * 512,
                                        [[2048, 128], [1, 512]])), f"s_xr{xr}",
                            waits=[xrl_free[xr]] if xrl_free[xr] else [])
                        tls.append((tgl, txl, ot))
                    grp = [banks.get(), banks.get()]
                    for kc in range(16):
                        for tt in range(2):
                            tg = p.op("tensor", lambda e, bk=grp[tt][1], sl=sl, kc=kc, tt=tt: e.matmul(
                                bk[:, :], wgu_b[:, sl, kc, :],
                                z_sb[:, kc, tt * 512:(tt + 1) * 512],
                                start=(kc == 0), stop=(kc == 15)),
                                waits=[tc, z_tok] if kc == 0 else [],
                                inc="pe" if (kc == 15 and tt == 1) else None)
                    for tt in range(2):
                        bi, bk = grp[tt]
                        tgl, txl, ot = tls[tt]
                        tv1 = p.op("vector", lambda e, bk=bk, ot=ot: e.tensor_tensor(
                            out=o_sb[:, ot, :], in0=bk[:, :], in1=g_rl[:, ot, :],
                            op=ALU.mult),
                            waits=[tg, tgl] + ([osb_free[ot]] if osb_free[ot] else []),
                            inc="v")
                        grl_free[ot] = tv1
                        tv2 = p.op("vector", lambda e, ot=ot, xr=ot % 2: e.tensor_tensor(
                            out=o_sb[:, ot, :], in0=o_sb[:, ot, :], in1=x_rl[:, xr, :],
                            op=ALU.add), waits=[txl, tv1], inc="v")
                        xrl_free[ot % 2] = tv2
                        banks.set_release(bi, tv1)
                        tst = p.dma(lambda e, m=m, hh=hh, tt=tt, ot=ot: e.dma_start(
                            out=bass.AP(dr["out_fm"],
                                        m * 128 * 2048 + hh * 1024 + tt * 512,
                                        [[2048, 128], [1, 512]]),
                            in_=o_sb[:, ot, :]), f"s_o{ot}", waits=[tv2])
                        osb_free[ot] = tst
                    slabs["gb"][sl] = tg
                p.barrier()

        for hh in range(2):
            emit_phase2(hh)

    for i in range(4):
        p.wait("sync", p.now(f"s_o{i}"))
    p.wait("sync", p.now("s_d"))
    p.replay()
    nc._keepalive = st
    return nc


_CACHE = {}


def kernel(**inputs):
    if "nc" not in _CACHE:
        _CACHE["nc"] = _build()
    nc = _CACHE["nc"]
    in_maps = _host_prep(inputs)
    res = run_bass_kernel_spmd(nc, in_maps, core_ids=list(range(NCORE)))
    return _assemble(res.results)
